# version 16
# baseline (speedup 1.0000x reference)
import math
import numpy as np
import concourse.bass as bass
import concourse.mybir as mybir
from concourse.bass_utils import run_bass_kernel_spmd

F32 = mybir.dt.float32
BF16 = mybir.dt.bfloat16
AF = mybir.ActivationFunctionType
ALU = mybir.AluOpType
AX = mybir.AxisListType

B, S, D, H, DH, F = 16, 2048, 1024, 8, 128, 4096
NCORES = 8
SPC = B // NCORES
NB = S // 128
NT = S // 512
DC = D // 128
FC = F // 128
OUTER = 512
NOUT = S // OUTER
TPO = OUTER // 128
GRP = 8
NGRP = FC // GRP
LN_EPS = 1e-5
GN_EPS = 1e-6

ENGS = ("sync", "gpsimd", "tensor", "vector", "scalar")
_PHASES = 99


def build_program():
    nc = bass.Bass("TRN2", target_bir_lowering=False)

    x_in = nc.dram_tensor("x", [SPC, S, D], F32, kind="ExternalInput")
    wq_d = nc.dram_tensor("wq", [D, D], F32, kind="ExternalInput")
    wk_d = nc.dram_tensor("wk", [D, D], F32, kind="ExternalInput")
    wv_d = nc.dram_tensor("wv", [D, D], F32, kind="ExternalInput")
    wg_d = nc.dram_tensor("wg", [D, D], F32, kind="ExternalInput")
    wo_d = nc.dram_tensor("wo", [D, D], F32, kind="ExternalInput")
    w1_d = nc.dram_tensor("w1", [D, F], F32, kind="ExternalInput")
    b1_d = nc.dram_tensor("b1", [F], F32, kind="ExternalInput")
    w2_d = nc.dram_tensor("w2", [F, D], F32, kind="ExternalInput")
    b2_d = nc.dram_tensor("b2", [D], F32, kind="ExternalInput")
    gam_d = nc.dram_tensor("gam", [D], F32, kind="ExternalInput")
    bet_d = nc.dram_tensor("bet", [D], F32, kind="ExternalInput")
    y_out = nc.dram_tensor("y", [SPC, S, D], F32, kind="ExternalOutput")

    ident_dr = nc.inline_tensor(np.eye(128, dtype=np.float32), "ident").ap()
    dt_d = nc.dram_tensor("dt", [H, 16, 128, 512], F32, kind="ExternalInput")

    arenaA = nc.alloc_sbuf_tensor("arenaA", [128, 16384], F32).ap()
    arenaB = nc.alloc_sbuf_tensor("arenaB", [128, 16384], F32).ap()
    arenaC = nc.alloc_sbuf_tensor("arenaC", [128, 14336], F32).ap()
    sb_dt = nc.alloc_sbuf_tensor("sb_dt", [128, 1024], F32).ap()
    sb_id = nc.alloc_sbuf_tensor("sb_id", [128, 128], F32).ap()
    sb_b1 = nc.alloc_sbuf_tensor("sb_b1", [128, FC], F32).ap()
    sb_st = nc.alloc_sbuf_tensor("sb_st", [128, 288], F32).ap()
    sb_rep = nc.alloc_sbuf_tensor("sb_rep", [128, 1024], F32).ap()
    scratch = nc.alloc_sbuf_tensor("scratch", [128, 2048], F32).ap()
    xrebuf = nc.alloc_sbuf_tensor("xrebuf", [128, 1024], F32).ap()
    PS = nc.alloc_psum_tensor("PS", [128, 4096], F32).ap()

    def bank(i, w=512, off=0):
        return PS[:, i * 512 + off: i * 512 + off + w]

    C_q = arenaC[:, 0:2048]
    C_k = arenaC[:, 2048:4096]
    C_v = arenaC[:, 4096:6144]
    C_Y = arenaC[:, 6144:8192]
    C_Sp = [arenaC[:, 8192 + i * 512: 8192 + (i + 1) * 512] for i in range(2)]
    C_wqkv = arenaC[:, 9216:12288]
    C_w8k = arenaC[:, 0:8192]
    C_hg = arenaC[:, 0:4096]
    C_w2g = arenaC[:, 4096:12288]
    C_w1b = [arenaC[:, 12288 + i * 1024: 12288 + (i + 1) * 1024] for i in range(2)]

    ST_SUM, ST_SQ, ST_MEAN, ST_M2, ST_VAR, ST_STD, ST_RSTD, ST_NMR = \
        0, 16, 32, 48, 64, 80, 96, 112
    ST_R2, ST_N2 = 128, 144
    XS = 160

    ops = {e: [] for e in ENGS}
    DL = [f"D{i}" for i in range(8)]
    EL = ["E0", "E1"]
    OL = ["O0", "O1"]
    cnt = {k: 0 for k in (DL + EL + OL + ["T", "V", "A", "G"])}
    lane_idx = {"sync": 0, "gpsimd": 0, "out": 0}
    last_wait = {e: {} for e in ENGS}
    sem = {}
    bankc = {i: None for i in range(8)}

    def wait(ename, waits):
        lw = last_wait[ename]
        todo = []
        for t in waits:
            if t is None:
                continue
            sname, v = t
            if v <= 0 or lw.get(sname, 0) >= v:
                continue
            lw[sname] = v
            todo.append((sname, v))
        if todo:
            def f(eng, todo=tuple(todo)):
                for sname, v in todo:
                    eng.wait_ge(sem[sname], v)
            ops[ename].append(f)

    def dma(ename, dst, src, waits=(), semname=None, slow=False, accum=False):
        if semname == "O":
            sm = OL[lane_idx["out"] % len(OL)]
            lane_idx["out"] += 1
        elif ename == "sync":
            sm = DL[lane_idx["sync"] % len(DL)]
            lane_idx["sync"] += 1
        else:
            sm = EL[lane_idx["gpsimd"] % len(EL)]
            lane_idx["gpsimd"] += 1
        wait(ename, list(waits) + [(sm, cnt[sm])])
        cnt[sm] += 16
        tok = (sm, cnt[sm])

        def f(eng, dst=dst, src=src, sm=sm, slow=slow, accum=accum):
            kw = {"accum_op": ALU.add} if accum else {}
            eng.dma_start(dst, src, allow_slow_non_contiguous=slow,
                          **kw).then_inc(sem[sm], 16)
        ops[ename].append(f)
        return tok

    def mm(out, lhsT, rhs, start, stop, waits=(), inc=False):
        wait("tensor", list(waits))
        tok = None
        if inc:
            cnt["T"] += 1
            tok = ("T", cnt["T"])

        def f(eng, out=out, lhsT=lhsT, rhs=rhs, start=start, stop=stop, tok=tok):
            i = eng.matmul(out, lhsT, rhs, start=start, stop=stop)
            if tok is not None:
                i.then_inc(sem["T"], 1)
        ops["tensor"].append(f)
        return tok

    def transpose(out, in_, waits=(), inc=True):
        wait("tensor", list(waits))
        cnt["T"] += 1
        tok = ("T", cnt["T"])

        def f(eng, out=out, in_=in_, tok=tok):
            eng.transpose(out, in_, sb_id).then_inc(sem["T"], 1)
        ops["tensor"].append(f)
        return tok

    def _ev(ename, sname, kind, *args, waits=(), inc=False, drain=False, **kw):
        wait(ename, list(waits))
        tok = None
        if inc:
            cnt[sname] += 1
            tok = (sname, cnt[sname])

        def f(eng, kind=kind, args=args, kw=kw, tok=tok, drain=drain):
            if drain:
                eng.drain()
            i = getattr(eng, kind)(*args, **kw)
            if tok is not None:
                i.then_inc(sem[tok[0]], 1)
        ops[ename].append(f)
        return tok

    def vop(kind, *a, **kw):
        return _ev("vector", "V", kind, *a, **kw)

    def aop(kind, *a, **kw):
        return _ev("scalar", "A", kind, *a, **kw)

    def gop(kind, *a, **kw):
        return _ev("gpsimd", "G", kind, *a, **kw)

    def snap_all():
        return [(k, cnt[k]) for k in cnt]

    def vasnap():
        return [("V", cnt["V"]), ("A", cnt["A"])]

    def stat_finish(ncols, col0, eps, inv, waits, base=0):
        sl = lambda arr: sb_st[:, base + arr + col0: base + arr + col0 + ncols]
        vop("tensor_scalar", sl(ST_MEAN), sl(ST_SUM), inv, None, ALU.mult,
            waits=waits, drain=True)
        vop("tensor_scalar", sl(ST_M2), sl(ST_SQ), inv, float(eps), ALU.mult,
            ALU.add)
        vop("tensor_tensor", sl(ST_VAR), sl(ST_MEAN), sl(ST_MEAN), ALU.mult,
            drain=True)
        tv = vop("tensor_tensor", sl(ST_VAR), sl(ST_M2), sl(ST_VAR),
                 ALU.subtract, drain=True, inc=True)
        ta = aop("activation", sl(ST_STD), sl(ST_VAR), AF.Sqrt,
                 waits=[tv], inc=True)
        vop("reciprocal", sl(ST_RSTD), sl(ST_STD), waits=[ta], drain=True)
        tok = vop("scalar_tensor_tensor", sl(ST_NMR), sl(ST_MEAN), -1.0,
                  sl(ST_RSTD), ALU.mult, ALU.mult, drain=True, inc=True)
        return tok

    def rstd_ap(c, base=0):
        return sb_st[:, base + ST_RSTD + c: base + ST_RSTD + c + 1]

    def nmr_ap(c, base=0):
        return sb_st[:, base + ST_NMR + c: base + ST_NMR + c + 1]

    t_consts = [
        dma("sync", sb_id, ident_dr),
        dma("sync", sb_b1, bass.AP(b1_d, 0, [[1, 128], [128, FC]]), slow=True),
    ]

    out_dma_toks = []
    prev_seq_end = []

    for s in range(SPC):
        apply_toks = [None] * NB
        a1sq_toks = [None] * NB
        tile_last_tr = [None] * NB
        for bb in range(NB):
            par = bb % 2
            xt = xrebuf[:, 0:1024]
            xnt = scratch[:, par * 1024:(par + 1) * 1024]
            w = list(t_consts) + list(prev_seq_end)
            if bb >= 1:
                w.append(apply_toks[bb - 1])
                w.append(a1sq_toks[bb - 1])
            src = bass.AP(x_in, s * S * D + bb * 128 * D, [[D, 128], [1, D]])
            tl = dma("sync", xt, src, waits=w)
            tsum = vop("reduce_sum", sb_st[:, ST_SUM + bb:ST_SUM + bb + 1], xt,
                       AX.X, waits=[tl] + list(prev_seq_end), inc=True)
            wsq = [tl] + list(prev_seq_end)
            if bb >= 2:
                wsq.append(tile_last_tr[bb - 2])
            tsq = aop("activation", xnt, xt, AF.Square,
                      accum_out=sb_st[:, ST_SQ + bb:ST_SQ + bb + 1],
                      waits=wsq, inc=True)
            a1sq_toks[bb] = tsq
            tfin = stat_finish(1, bb, LN_EPS, 1.0 / D, [tsum, tsq])
            tap = vop("tensor_scalar", xnt, xt, rstd_ap(bb), nmr_ap(bb),
                      ALU.mult, ALU.add, waits=[tfin], drain=True, inc=True)
            apply_toks[bb] = tap
            for c in range(DC):
                bk = (bb * DC + c) % 4
                ttr = transpose(bank(bk, 128), xnt[:, c * 128:(c + 1) * 128],
                                waits=[tap, bankc[bk]])
                dstA = arenaA[:, c * 2048 + bb * 128: c * 2048 + (bb + 1) * 128]
                if c % 2 == 0:
                    tev = vop("tensor_copy", dstA, bank(bk, 128), waits=[ttr],
                              inc=True)
                else:
                    tev = aop("copy", dstA, bank(bk, 128), waits=[ttr], inc=True)
                bankc[bk] = tev
                tile_last_tr[bb] = ttr
        xn1T_ready = vasnap()

        if _PHASES < 2:
            continue
        wg_toks = []
        for c in range(DC):
            src = bass.AP(wg_d, c * 128 * D, [[D, 128], [1, D]])
            wg_toks.append(dma("gpsimd", C_w8k[:, c * 1024:(c + 1) * 1024], src,
                               waits=list(prev_seq_end)))
        for t in range(NB):
            pr = t % 2
            bk0 = 4 + 2 * pr
            wts = xn1T_ready + wg_toks + [bankc[bk0], bankc[bk0 + 1]]
            for c in range(DC):
                for hh in range(2):
                    mm(bank(bk0 + hh),
                       arenaA[:, c * 2048 + t * 128: c * 2048 + (t + 1) * 128],
                       C_w8k[:, c * 1024 + hh * 512: c * 1024 + hh * 512 + 512],
                       start=(c == 0), stop=(c == DC - 1),
                       waits=wts if (c == 0 and hh == 0) else (),
                       inc=(c == DC - 1 and hh == 1))
            tmm = ("T", cnt["T"])
            tev = aop("activation", arenaB[:, t * 1024:(t + 1) * 1024],
                      bank(bk0, 1024), AF.Silu,
                      waits=[tmm] + list(prev_seq_end), inc=True)
            bankc[bk0] = tev
            bankc[bk0 + 1] = tev
        g_done_A = ("A", cnt["A"])

        if _PHASES < 3:
            continue
        prev_head_zmult = None
        sp_cons = [None, None]
        dt_prev_cons = [None, None]
        last_q_mm = None
        zmult_toks = []

        def load_wqkv(h, w):
            toks = []
            for wi, wd in enumerate((wq_d, wk_d, wv_d)):
                srcw = bass.AP(wd, h * 128, [[D, 128], [128 * D, DC], [1, 128]])
                dst = bass.AP(C_wqkv.tensor, C_wqkv.offset + wi * 1024,
                              [[14336, 128], [128, DC], [1, 128]])
                toks.append(dma("gpsimd", dst, srcw, waits=w))
            return toks

        wqkv_toks = load_wqkv(0, list(prev_seq_end))
        for h in range(H):
            wq_t, wk_t, wv_t = wqkv_toks

            def wslice(wi, c):
                off = wi * 1024 + c * 128
                return C_wqkv[:, off:off + 128]

            v_ev = []
            for mb in range(NB):
                pr = mb % 2
                wts = [wv_t, bankc[pr]] + xn1T_ready
                for c in range(DC):
                    mm(bank(pr, 128),
                       arenaA[:, c * 2048 + mb * 128: c * 2048 + (mb + 1) * 128],
                       wslice(2, c), start=(c == 0), stop=(c == DC - 1),
                       waits=wts if c == 0 else (), inc=(c == DC - 1))
                tmm = ("T", cnt["T"])
                tev = aop("copy", C_v[:, mb * 128:(mb + 1) * 128],
                          bank(pr, 128), waits=[tmm], inc=True)
                bankc[pr] = tev
                v_ev.append(tev)
            k_ev = []
            for ti in range(NT):
                pr = ti % 2
                wts = [wk_t, bankc[2 + pr]] + xn1T_ready
                for c in range(DC):
                    mm(bank(2 + pr), wslice(1, c),
                       arenaA[:, c * 2048 + ti * 512: c * 2048 + ti * 512 + 512],
                       start=(c == 0), stop=(c == DC - 1),
                       waits=wts if c == 0 else (), inc=(c == DC - 1))
                tmm = ("T", cnt["T"])
                tev = aop("copy", C_k[:, ti * 512:(ti + 1) * 512], bank(2 + pr),
                          waits=[tmm], inc=True)
                bankc[2 + pr] = tev
                k_ev.append(tev)
            q_ev = []
            for ti in range(NT):
                pr = ti % 2
                wts = [wq_t, bankc[2 + pr]] + xn1T_ready
                tq = None
                for c in range(DC):
                    tq = mm(bank(2 + pr), wslice(0, c),
                            arenaA[:, c * 2048 + ti * 512: c * 2048 + ti * 512 + 512],
                            start=(c == 0), stop=(c == DC - 1),
                            waits=wts if c == 0 else (), inc=(c == DC - 1))
                last_q_mm = tq
                tev = vop("tensor_scalar", C_q[:, ti * 512:(ti + 1) * 512],
                          bank(2 + pr), float(np.float32(1.0 / math.sqrt(DH))),
                          None, ALU.mult, waits=[tq], inc=True)
                bankc[2 + pr] = tev
                q_ev.append(tev)
            if h + 1 < H:
                wqkv_toks = load_wqkv(h + 1, [last_q_mm])

            gn_sum_toks = []
            SCB = (0, 1, 4)
            for nt in range(NT):
                nmb = 4 * nt + 4
                rbk = 2 + (nt % 2)
                sc_toks = [None] * nmb
                ev_toks = [None] * nmb
                av_toks = [None] * nmb
                dt_toks = [None] * nmb

                def emit_dtload(mb, nt=nt, ev_toks=ev_toks, dt_toks=dt_toks):
                    pr = mb % 2
                    ridx = 4 * nt - mb + 3
                    w = [ev_toks[mb - 2] if mb >= 2 else dt_prev_cons[pr]]
                    dt_toks[mb] = dma(
                        "sync", sb_dt[:, pr * 512:(pr + 1) * 512],
                        bass.AP(dt_d, (h * 16 + ridx) * 128 * 512,
                                [[512, 128], [1, 512]]), waits=w)

                def emit_scores(mb, nt=nt, sc_toks=sc_toks, ev_toks=ev_toks):
                    bk = SCB[mb % 3]
                    wts = [q_ev[nt], k_ev[mb // 4], bankc[bk]]
                    sc_toks[mb] = mm(bank(bk), C_k[:, mb * 128:(mb + 1) * 128],
                                     C_q[:, nt * 512: nt * 512 + 512],
                                     start=True, stop=True, waits=wts, inc=True)

                def emit_evac(mb, nt=nt, sc_toks=sc_toks, ev_toks=ev_toks,
                              dt_toks=dt_toks):
                    bk = SCB[mb % 3]
                    pr = mb % 2
                    wts = [sc_toks[mb], sp_cons[pr], dt_toks[mb]]
                    ev_toks[mb] = vop("tensor_tensor", C_Sp[pr], bank(bk),
                                      sb_dt[:, pr * 512:(pr + 1) * 512],
                                      ALU.mult, waits=wts, inc=True)
                    bankc[bk] = ev_toks[mb]
                    dt_prev_cons[pr] = ev_toks[mb]

                def emit_av(mb, nt=nt, nmb=nmb, rbk=rbk, ev_toks=ev_toks,
                            av_toks=av_toks):
                    pr = mb % 2
                    wts = [ev_toks[mb], v_ev[mb]]
                    if mb == 0:
                        wts += [bankc[rbk]]
                    tk = None
                    for j in range(4):
                        tk = mm(bank(rbk, 128, off=j * 128),
                                C_Sp[pr][:, j * 128:(j + 1) * 128],
                                C_v[:, mb * 128:(mb + 1) * 128],
                                start=(mb == 0 and j == 0),
                                stop=(mb == nmb - 1 and j == 3),
                                waits=wts if j == 0 else (), inc=(j == 3))
                    av_toks[mb] = tk
                    sp_cons[pr] = tk

                emit_dtload(0)
                emit_dtload(1) if nmb > 1 else None
                emit_scores(0)
                if nmb > 1:
                    emit_scores(1)
                for mb in range(nmb):
                    emit_evac(mb)
                    if mb + 2 < nmb:
                        emit_dtload(mb + 2)
                    if mb + 2 < nmb:
                        emit_scores(mb + 2)
                    emit_av(mb)
                t1last = None
                for j in range(4):
                    nbk = 4 * nt + j
                    yb = C_Y[:, nbk * 128:(nbk + 1) * 128]
                    t1 = aop("activation", yb, bank(rbk, 128, off=j * 128),
                             AF.Copy,
                             accum_out=sb_st[:, ST_SUM + nbk:ST_SUM + nbk + 1],
                             waits=[av_toks[nmb - 1], prev_head_zmult],
                             inc=True)
                    t2 = aop("activation", scratch[:, 0:128], yb, AF.Square,
                             accum_out=sb_st[:, ST_SQ + nbk:ST_SQ + nbk + 1],
                             drain=True, inc=True)
                    gn_sum_toks.append(t2)
                    t1last = t1
                bankc[rbk] = t1last
            tfin = stat_finish(NB, 0, GN_EPS, 1.0 / DH, gn_sum_toks)
            apv = apg = None
            for nbk in range(NB):
                yb = C_Y[:, nbk * 128:(nbk + 1) * 128]
                if nbk % 2 == 0:
                    apv = vop("tensor_scalar", yb, yb, rstd_ap(nbk),
                              nmr_ap(nbk), ALU.mult, ALU.add, waits=[tfin],
                              drain=(nbk == 0), inc=(nbk == NB - 2))
                else:
                    apg = gop("tensor_scalar", yb, yb, rstd_ap(nbk),
                              nmr_ap(nbk), ALU.mult, ALU.add, waits=[tfin],
                              drain=(nbk == 1), inc=(nbk == NB - 1))
            gslice = bass.AP(arenaB.tensor, arenaB.offset + h * 128,
                             [[16384, 128], [1024, NB], [1, 128]])
            yv = bass.AP(C_Y.tensor, C_Y.offset,
                         [[14336, 128], [128, NB], [1, 128]])
            prev_head_zmult = gop("tensor_tensor", gslice, gslice, yv, ALU.mult,
                                  waits=[apv, apg, g_done_A], drain=True,
                                  inc=True)
            zmult_toks.append(prev_head_zmult)

        if _PHASES < 4:
            continue
        head_T_done = ("T", cnt["T"])
        wo_toks = []
        for c in range(DC):
            src = bass.AP(wo_d, c * 128 * D, [[D, 128], [1, D]])
            wo_toks.append(dma("gpsimd", C_w8k[:, c * 1024:(c + 1) * 1024], src,
                               waits=[head_T_done] + vasnap()))
        zT_ev = {}
        for t in range(NB):
            for c in range(DC):
                bk = (t * DC + c) % 4
                ttr = transpose(
                    bank(bk, 128),
                    arenaB[:, t * 1024 + c * 128: t * 1024 + c * 128 + 128],
                    waits=[zmult_toks[-1], bankc[bk]])
                dstA = arenaA[:, c * 2048 + t * 128: c * 2048 + (t + 1) * 128]
                if c % 2 == 0:
                    tev = vop("tensor_copy", dstA, bank(bk, 128), waits=[ttr],
                              inc=True)
                else:
                    tev = aop("copy", dstA, bank(bk, 128), waits=[ttr], inc=True)
                bankc[bk] = tev
                zT_ev[t] = tev
        zT_ready = vasnap()

        x1_toks = [None] * NB
        sq_toks = [None] * NB
        for t in range(NB):
            pr = t % 2
            bk0 = 4 + 2 * pr
            wts = zT_ready + wo_toks + [bankc[bk0], bankc[bk0 + 1]]
            for c in range(DC):
                for hh in range(2):
                    mm(bank(bk0 + hh),
                       arenaA[:, c * 2048 + t * 128: c * 2048 + (t + 1) * 128],
                       C_w8k[:, c * 1024 + hh * 512: c * 1024 + hh * 512 + 512],
                       start=(c == 0), stop=(c == DC - 1),
                       waits=wts if (c == 0 and hh == 0) else (),
                       inc=(c == DC - 1 and hh == 1))
            tmm = ("T", cnt["T"])
            tsum = vop("reduce_sum", sb_st[:, ST_SUM + t:ST_SUM + t + 1],
                       bank(bk0, 1024), AX.X, waits=[tmm], inc=True)
            lnt = scratch[:, pr * 1024:(pr + 1) * 1024]
            wsq = [tmm, tsum]
            if t >= 2:
                wsq.append(x1_toks[t - 2])
            elif t < 2 and s >= 0:
                wsq.append(apply_toks[NB - 2 + t])
            tsq = aop("activation", lnt, bank(bk0, 1024), AF.Square,
                      accum_out=sb_st[:, ST_SQ + t:ST_SQ + t + 1], waits=wsq,
                      inc=True)
            tfin = stat_finish(1, t, LN_EPS, 1.0 / D, [tsum, tsq])
            tln = vop("tensor_scalar", lnt, bank(bk0, 1024), rstd_ap(t),
                      nmr_ap(t), ALU.mult, ALU.add, waits=[tfin], drain=True,
                      inc=True)
            bankc[bk0] = tln
            bankc[bk0 + 1] = tln
            xr = xrebuf[:, 0:1024]
            wxr = [head_T_done]
            if t >= 1:
                wxr.append(x1_toks[t - 1])
                wxr.append(sq_toks[t - 1])
            else:
                wxr.append(apply_toks[NB - 1])
                wxr.append(a1sq_toks[NB - 1])
            txr = dma("gpsimd", xr,
                      bass.AP(x_in, s * S * D + t * 128 * D, [[D, 128], [1, D]]),
                      waits=wxr)
            x1_toks[t] = vop("scalar_tensor_tensor",
                             arenaB[:, t * 1024:(t + 1) * 1024], lnt, 1.0, xr,
                             ALU.mult, ALU.add,
                             accum_out=sb_st[:, XS + ST_SUM + t:XS + ST_SUM + t + 1],
                             waits=[tln, txr, zT_ev[t]], inc=True)
            sq_toks[t] = aop("activation", xr,
                             arenaB[:, t * 1024:(t + 1) * 1024], AF.Square,
                             accum_out=sb_st[:, XS + ST_SQ + t:XS + ST_SQ + t + 1],
                             waits=[x1_toks[t]], drain=True, inc=True)

        if _PHASES < 5:
            continue
        ffn_fin = stat_finish(NB, 0, LN_EPS, 1.0 / D,
                              [x1_toks[-1], sq_toks[-1]], base=XS)
        vop("tensor_copy", sb_st[:, ST_R2:ST_R2 + NB],
            sb_st[:, XS + ST_RSTD:XS + ST_RSTD + NB], waits=[ffn_fin],
            drain=True)
        r2n2_tok = vop("tensor_copy", sb_st[:, ST_N2:ST_N2 + NB],
                       sb_st[:, XS + ST_NMR:XS + ST_NMR + NB], inc=True)

        XS = 160
        ACCB = 8192
        grep = sb_dt[:, 0:1024]
        brep = sb_rep[:, 0:1024]
        g_tok = dma("gpsimd", grep,
                    bass.AP(gam_d, 0, [[0, 128], [1, D]]),
                    waits=vasnap() + [("T", cnt["T"])])
        b_tok = dma("gpsimd", brep,
                    bass.AP(bet_d, 0, [[0, 128], [1, D]]),
                    waits=vasnap())
        rep_ready = None

        def tile_xn2T(o, tt, prev_ltr):
            slab2 = (o % 2) * 4096
            tglob = o * TPO + tt
            x1t = arenaB[:, tglob * 1024:(tglob + 1) * 1024]
            par = tt % 2
            xnt = scratch[:, par * 1024:(par + 1) * 1024]
            wap = [x1_toks[tglob], sq_toks[tglob], r2n2_tok, prev_ltr,
                   ("V", cnt["V"]), ("A", cnt["A"])]
            tap = vop("tensor_scalar", xnt, x1t, rstd_ap(tglob, base=XS),
                      nmr_ap(tglob, base=XS), ALU.mult, ALU.add,
                      waits=wap, drain=True, inc=True)
            ltr = None
            for c in range(DC):
                bk = 2 + (tt * DC + c) % 2
                ltr = transpose(bank(bk, 128), xnt[:, c * 128:(c + 1) * 128],
                                waits=[tap, bankc[bk]])
                dstA = arenaA[:, slab2 + c * OUTER + tt * 128:
                              slab2 + c * OUTER + (tt + 1) * 128]
                if c % 2 == 0:
                    tev = vop("tensor_copy", dstA, bank(bk, 128), waits=[ltr],
                              inc=True)
                else:
                    tev = aop("copy", dstA, bank(bk, 128), waits=[ltr],
                              inc=True)
                bankc[bk] = tev
            return ltr

        def tail_tile(o, tt, acc_toks, xn2T_tr):
            nonlocal t3_prev, t8_prev
            tglob = o * TPO + tt
            sc = (o % 2) * 4
            zc = sc + tt
            x1t = arenaB[:, tglob * 1024:(tglob + 1) * 1024]
            accb = 8192 + (o % 2) * 4096
            acct = arenaA[:, accb + tt * 1024: accb + (tt + 1) * 1024]
            zt = scratch[:, 0:1024]
            ot = xrebuf[:, 0:1024]
            t1 = vop("tensor_scalar", zt, x1t,
                     rstd_ap(tglob, base=XS), nmr_ap(tglob, base=XS),
                     ALU.mult, ALU.add,
                     waits=list(acc_toks) + [g_tok, b_tok, t3_prev, xn2T_tr],
                     drain=True, inc=True)
            t2 = vop("scalar_tensor_tensor", zt, zt, 1.0, acct,
                     ALU.mult, ALU.add,
                     accum_out=sb_st[:, ST_SUM + zc:ST_SUM + zc + 1],
                     waits=[t1], drain=True, inc=True)
            t3_prev = t2
            wdump = [t2, t8_prev]
            if out_dma_toks:
                wdump.append(out_dma_toks[-1])
            t4 = aop("activation", ot, zt, AF.Square,
                     accum_out=sb_st[:, ST_SQ + zc:ST_SQ + zc + 1],
                     waits=wdump, drain=True, inc=True)
            t5 = stat_finish(1, zc, LN_EPS, 1.0 / D, [t2, t4])
            t6 = vop("tensor_scalar", zt, zt, rstd_ap(zc), nmr_ap(zc),
                     ALU.mult, ALU.add, waits=[t5, t8_prev, t4], drain=True,
                     inc=True)
            t7 = gop("tensor_tensor", zt, zt, grep, ALU.mult, waits=[t6],
                     drain=True, inc=True)
            t8 = vop("scalar_tensor_tensor", zt, zt, 1.0, brep,
                     ALU.mult, ALU.add,
                     accum_out=sb_st[:, ST_SUM + 8 + zc:ST_SUM + 9 + zc],
                     waits=[t7], drain=True, inc=True)
            t8_prev = t8
            t10 = aop("activation", ot, zt, AF.Square,
                      accum_out=sb_st[:, ST_SQ + 8 + zc:ST_SQ + 9 + zc],
                      waits=[t8, t4], drain=True, inc=True)
            t11 = stat_finish(1, 8 + zc, LN_EPS, 1.0 / D, [t8, t10])
            t12 = vop("tensor_scalar", ot, zt, rstd_ap(8 + zc), nmr_ap(8 + zc),
                      ALU.mult, ALU.add, waits=[t11, t10], drain=True, inc=True)
            t13 = vop("tensor_tensor", ot, ot, x1t, ALU.add, waits=[t12],
                      drain=True, inc=True)
            dst = bass.AP(y_out, s * S * D + (o * OUTER + tt * 128) * D,
                          [[D, 128], [1, D]])
            out_dma_toks.append(dma("gpsimd", dst, ot, waits=[t13],
                                    semname="O"))

        t3_prev = None
        t8_prev = None
        prev_ltr = None
        for tt0 in range(TPO):
            prev_ltr = tile_xn2T(0, tt0, prev_ltr)
        next_tr, next_ready = prev_ltr, vasnap()
        acc_last_of = {}
        w1_tok = [None, None]
        w1_free = [None, None]
        w2_free = [None] * GRP
        hslot_free = [None] * GRP
        for o in range(NOUT):
            slab = (o % 2) * 4096
            xn2T_ready = next_ready
            accb_o = 8192 + (o % 2) * 4096

            acc_last = None
            last_consume_T = None
            for g in range(NGRP):
                h_ready = [None] * GRP
                for fi in range(2):
                    fglob = g * GRP + fi
                    w1_tok[fi % 2] = dma(
                        "sync", C_w1b[fi % 2],
                        bass.AP(w1_d, fglob * 128,
                                [[F, 128], [128 * F, DC], [1, 128]]),
                        waits=[w1_free[fi % 2]])
                for fi in range(GRP):
                    fglob = g * GRP + fi
                    wp = fi % 2
                    bk0 = fi % 2
                    wts = xn2T_ready + [w1_tok[wp], bankc[bk0]]
                    tmm_ = None
                    for c in range(DC):
                        tmm_ = mm(bank(bk0),
                                  C_w1b[wp][:, c * 128:(c + 1) * 128],
                                  arenaA[:, slab + c * OUTER:
                                         slab + c * OUTER + OUTER],
                                  start=(c == 0), stop=(c == DC - 1),
                                  waits=wts if c == 0 else (),
                                  inc=(c == DC - 1))
                    w1_free[wp] = tmm_
                    if fi + 2 < GRP:
                        w1_tok[wp] = dma(
                            "sync", C_w1b[wp],
                            bass.AP(w1_d, (fglob + 2) * 128,
                                    [[F, 128], [128 * F, DC], [1, 128]]),
                            waits=[tmm_])
                    trl = aop("activation",
                              C_hg[:, fi * OUTER:(fi + 1) * OUTER],
                              bank(bk0, OUTER), AF.Relu,
                              bias=sb_b1[:, fglob:fglob + 1],
                              waits=[tmm_, hslot_free[fi]], inc=True)
                    bankc[bk0] = trl
                    h_ready[fi] = trl
                for fi in range(GRP):
                    fglob = g * GRP + fi
                    tw2 = dma("sync", C_w2g[:, fi * 1024:(fi + 1) * 1024],
                              bass.AP(w2_d, fglob * 128 * D, [[D, 128], [1, D]]),
                              waits=[w2_free[fi]])
                    w2_free[fi] = tw2
                for hh in range(2):
                    for ns in range(OUTER // 256):
                        pp = ns % 2
                        bk0 = 4 + 2 * pp
                        wts = [bankc[bk0], bankc[bk0 + 1]] + h_ready + \
                            [w2_free[x] for x in range(GRP)]
                        tmm_ = None
                        for fi in range(GRP):
                            for nb2 in range(2):
                                tmm_ = mm(
                                    bank(bk0 + nb2),
                                    C_hg[:, fi * OUTER + (ns * 2 + nb2) * 128:
                                         fi * OUTER + (ns * 2 + nb2) * 128 + 128],
                                    C_w2g[:, fi * 1024 + hh * 512:
                                          fi * 1024 + hh * 512 + 512],
                                    start=(fi == 0), stop=(fi == GRP - 1),
                                    waits=wts if (fi == 0 and nb2 == 0) else (),
                                    inc=(fi == GRP - 1 and nb2 == 1))
                        last_consume_T = tmm_
                        for nb2 in range(2):
                            base = accb_o + (ns * 2 + nb2) * 1024 + hh * 512
                            dst = arenaA[:, base: base + 512]
                            if g == 0:
                                ev = vop("tensor_copy", dst, bank(bk0 + nb2),
                                         waits=[last_consume_T], inc=True)
                            else:
                                ev = vop("tensor_tensor", dst, dst,
                                         bank(bk0 + nb2), ALU.add,
                                         waits=[last_consume_T], inc=True)
                            acc_last = ev
                            bankc[bk0 + nb2] = ev
                for fi in range(GRP):
                    hslot_free[fi] = last_consume_T
                    w2_free[fi] = last_consume_T
                if o + 1 < NOUT:
                    next_tr = tile_xn2T(o + 1, g, next_tr)
                if o >= 1:
                    tail_tile(o - 1, g, acc_last_of[o - 1], next_tr)
            atoks = []
            for tt2 in range(TPO):
                atoks.append(dma(
                    "gpsimd",
                    bass.AP(arenaA.tensor, accb_o + tt2 * 1024,
                            [[16384, 128], [1, 1024]]),
                    bass.AP(b2_d, 0, [[0, 128], [1, 1024]]),
                    waits=[acc_last if tt2 == 0 else None], accum=True))
            acc_last_of[o] = atoks
            if o + 1 < NOUT:
                next_ready = vasnap()
        for tt in range(TPO):
            tail_tile(NOUT - 1, tt, acc_last_of[NOUT - 1], next_tr)
        prev_seq_end = snap_all()

    if _PHASES < 5:
        t_end = snap_all()
        for s2 in range(SPC):
            for t2 in range(NB):
                d2 = bass.AP(y_out, s2 * S * D + t2 * 128 * D, [[D, 128], [1, D]])
                dma("gpsimd", d2, arenaB[:, t2 * 1024:(t2 + 1) * 1024],
                    waits=t_end, semname="O")
    fin_counts = [(k, cnt[k]) for k in OL]

    def fin(eng, fc=tuple(fin_counts)):
        for k, n in fc:
            eng.wait_ge(sem[k], n)
    ops["gpsimd"].append(fin)

    import contextlib
    with contextlib.ExitStack() as stack:
        for k in list(cnt.keys()):
            sem[k] = stack.enter_context(nc.semaphore(k))
        block = stack.enter_context(nc.Block())

        @block.sync
        def _(eng):
            for f in ops["sync"]:
                f(eng)

        @block.gpsimd
        def _(eng):
            for f in ops["gpsimd"]:
                f(eng)

        @block.tensor
        def _(eng):
            for f in ops["tensor"]:
                f(eng)

        @block.vector
        def _(eng):
            for f in ops["vector"]:
                f(eng)

        @block.scalar
        def _(eng):
            for f in ops["scalar"]:
                f(eng)

    return nc


def _decay_tiles():
    import jax
    import jax.numpy as jnp
    try:
        cpu = jax.devices("cpu")[0]
        with jax.default_device(cpu):
            gammas = 1.0 - jnp.exp2(-5.0 - jnp.arange(H, dtype=jnp.float32))
            log_g = jnp.log(gammas)
            diffs = jnp.arange(S, dtype=jnp.float32)
            dv = np.asarray(jax.device_get(
                jnp.exp(log_g[:, None] * diffs[None, :])), dtype=np.float32)
    except Exception:
        g32 = (1.0 - np.exp2(-5.0 - np.arange(H, dtype=np.float32))).astype(
            np.float32)
        lg = np.log(g32).astype(np.float32)
        dfs = np.arange(S, dtype=np.float32)
        dv = np.exp(lg[:, None] * dfs[None, :]).astype(np.float32)
    tiles = np.zeros((H, 16, 128, 512), dtype=np.float32)
    ml = np.arange(128)[:, None]
    nl = np.arange(512)[None, :]
    for ridx in range(16):
        r = ridx * 128 - 384
        d = r + nl - ml
        valid = d >= 0
        dcl = np.clip(d, 0, S - 1)
        for h in range(H):
            tiles[h, ridx] = np.where(valid, dv[h][dcl], 0.0)
    return tiles


def kernel(**inputs):
    x = np.ascontiguousarray(np.asarray(inputs["L0_attn_X"], dtype=np.float32))
    wq = np.ascontiguousarray(np.asarray(inputs["L0_attn_Wq"], dtype=np.float32))
    wk = np.ascontiguousarray(np.asarray(inputs["L0_attn_Wk"], dtype=np.float32))
    wv = np.ascontiguousarray(np.asarray(inputs["L0_attn_Wv"], dtype=np.float32))
    wg = np.ascontiguousarray(np.asarray(inputs["L0_attn_W_G"], dtype=np.float32))
    wo = np.ascontiguousarray(np.asarray(inputs["L0_attn_W_O"], dtype=np.float32))
    w1 = np.ascontiguousarray(np.asarray(inputs["L0_ffn_w1"][0], dtype=np.float32))
    b1 = np.ascontiguousarray(np.asarray(inputs["L0_ffn_b1"][0], dtype=np.float32))
    w2 = np.ascontiguousarray(np.asarray(inputs["L0_ffn_w2"][0], dtype=np.float32))
    b2 = np.ascontiguousarray(np.asarray(inputs["L0_ffn_b2"][0], dtype=np.float32))
    gam = np.ascontiguousarray(np.asarray(inputs["L0_ffn_gamma"], dtype=np.float32))
    bet = np.ascontiguousarray(np.asarray(inputs["L0_ffn_beta"], dtype=np.float32))

    nc = build_program()
    dt = _decay_tiles()
    in_maps = []
    for c in range(NCORES):
        in_maps.append({
            "x": np.ascontiguousarray(x[c * SPC:(c + 1) * SPC]),
            "wq": wq, "wk": wk, "wv": wv, "wg": wg, "wo": wo,
            "w1": w1, "b1": b1, "w2": w2, "b2": b2, "gam": gam, "bet": bet,
            "dt": dt,
        })
    res = run_bass_kernel_spmd(nc, in_maps, core_ids=list(range(NCORES)))
    out = np.concatenate([r["y"] for r in res.results], axis=0)
    return out.astype(np.float32)


# version 18
# speedup vs baseline: 1.0788x; 1.0788x over previous
import math
import numpy as np
import concourse.bass as bass
import concourse.mybir as mybir
from concourse.bass_utils import run_bass_kernel_spmd

F32 = mybir.dt.float32
BF16 = mybir.dt.bfloat16
AF = mybir.ActivationFunctionType
ALU = mybir.AluOpType
AX = mybir.AxisListType

B, S, D, H, DH, F = 16, 2048, 1024, 8, 128, 4096
NCORES = 8
SPC = B // NCORES
NB = S // 128
NT = S // 512
DC = D // 128
FC = F // 128
OUTER = 512
NOUT = S // OUTER
TPO = OUTER // 128
GRP = 8
NGRP = FC // GRP
LN_EPS = 1e-5
GN_EPS = 1e-6

ENGS = ("sync", "gpsimd", "tensor", "vector", "scalar")
_PHASES = 99


def build_program():
    nc = bass.Bass("TRN2", target_bir_lowering=False)

    x_in = nc.dram_tensor("x", [SPC, S, D], F32, kind="ExternalInput")
    wq_d = nc.dram_tensor("wq", [D, D], F32, kind="ExternalInput")
    wk_d = nc.dram_tensor("wk", [D, D], F32, kind="ExternalInput")
    wv_d = nc.dram_tensor("wv", [D, D], F32, kind="ExternalInput")
    wg_d = nc.dram_tensor("wg", [D, D], F32, kind="ExternalInput")
    wo_d = nc.dram_tensor("wo", [D, D], F32, kind="ExternalInput")
    w1_d = nc.dram_tensor("w1", [D, F], F32, kind="ExternalInput")
    b1_d = nc.dram_tensor("b1", [F], F32, kind="ExternalInput")
    w2_d = nc.dram_tensor("w2", [F, D], F32, kind="ExternalInput")
    b2_d = nc.dram_tensor("b2", [D], F32, kind="ExternalInput")
    gam_d = nc.dram_tensor("gam", [D], F32, kind="ExternalInput")
    bet_d = nc.dram_tensor("bet", [D], F32, kind="ExternalInput")
    y_out = nc.dram_tensor("y", [SPC, S, D], F32, kind="ExternalOutput")

    ident_dr = nc.inline_tensor(np.eye(128, dtype=np.float32), "ident").ap()
    dt_d = nc.dram_tensor("dt", [H, 16, 128, 512], F32, kind="ExternalInput")

    arenaA = nc.alloc_sbuf_tensor("arenaA", [128, 16384], F32).ap()
    arenaB = nc.alloc_sbuf_tensor("arenaB", [128, 16384], F32).ap()
    arenaC = nc.alloc_sbuf_tensor("arenaC", [128, 14336], F32).ap()
    sb_dt = nc.alloc_sbuf_tensor("sb_dt", [128, 1024], F32).ap()
    sb_id = nc.alloc_sbuf_tensor("sb_id", [128, 128], F32).ap()
    sb_b1 = nc.alloc_sbuf_tensor("sb_b1", [128, FC], F32).ap()
    sb_st = nc.alloc_sbuf_tensor("sb_st", [128, 288], F32).ap()
    scratch = nc.alloc_sbuf_tensor("scratch", [128, 2048], F32).ap()
    xrebuf = nc.alloc_sbuf_tensor("xrebuf", [128, 2048], F32).ap()
    PS = nc.alloc_psum_tensor("PS", [128, 4096], F32).ap()

    def bank(i, w=512, off=0):
        return PS[:, i * 512 + off: i * 512 + off + w]

    C_q = arenaC[:, 0:2048]
    C_k = arenaC[:, 2048:4096]
    C_v = arenaC[:, 4096:6144]
    C_Y = arenaC[:, 6144:8192]
    C_Sp = [arenaC[:, 8192 + i * 512: 8192 + (i + 1) * 512] for i in range(2)]
    C_wqkv = arenaC[:, 9216:12288]
    C_w8k = arenaC[:, 0:8192]
    C_hg = arenaC[:, 0:4096]
    C_w2g = arenaC[:, 4096:12288]
    C_w1b = [arenaC[:, 12288 + i * 1024: 12288 + (i + 1) * 1024] for i in range(2)]

    ST_SUM, ST_SQ, ST_MEAN, ST_M2, ST_VAR, ST_STD, ST_RSTD, ST_NMR = \
        0, 16, 32, 48, 64, 80, 96, 112
    ST_R2, ST_N2 = 128, 144
    XS = 160

    ops = {e: [] for e in ENGS}
    DL = [f"D{i}" for i in range(8)]
    EL = ["E0", "E1"]
    OL = ["O0", "O1"]
    cnt = {k: 0 for k in (DL + EL + OL + ["T", "V", "A", "G"])}
    lane_idx = {"sync": 0, "gpsimd": 0, "out": 0}
    last_wait = {e: {} for e in ENGS}
    sem = {}
    bankc = {i: None for i in range(8)}

    def wait(ename, waits):
        lw = last_wait[ename]
        todo = []
        for t in waits:
            if t is None:
                continue
            sname, v = t
            if v <= 0 or lw.get(sname, 0) >= v:
                continue
            lw[sname] = v
            todo.append((sname, v))
        if todo:
            def f(eng, todo=tuple(todo)):
                for sname, v in todo:
                    eng.wait_ge(sem[sname], v)
            ops[ename].append(f)

    def dma(ename, dst, src, waits=(), semname=None, slow=False, accum=False):
        if semname == "O":
            sm = OL[lane_idx["out"] % len(OL)]
            lane_idx["out"] += 1
        elif ename == "sync":
            sm = DL[lane_idx["sync"] % len(DL)]
            lane_idx["sync"] += 1
        else:
            sm = EL[lane_idx["gpsimd"] % len(EL)]
            lane_idx["gpsimd"] += 1
        wait(ename, list(waits) + [(sm, cnt[sm])])
        cnt[sm] += 16
        tok = (sm, cnt[sm])

        def f(eng, dst=dst, src=src, sm=sm, slow=slow, accum=accum):
            kw = {"accum_op": ALU.add} if accum else {}
            eng.dma_start(dst, src, allow_slow_non_contiguous=slow,
                          **kw).then_inc(sem[sm], 16)
        ops[ename].append(f)
        return tok

    def mm(out, lhsT, rhs, start, stop, waits=(), inc=False):
        wait("tensor", list(waits))
        tok = None
        if inc:
            cnt["T"] += 1
            tok = ("T", cnt["T"])

        def f(eng, out=out, lhsT=lhsT, rhs=rhs, start=start, stop=stop, tok=tok):
            i = eng.matmul(out, lhsT, rhs, start=start, stop=stop)
            if tok is not None:
                i.then_inc(sem["T"], 1)
        ops["tensor"].append(f)
        return tok

    def transpose(out, in_, waits=(), inc=True):
        wait("tensor", list(waits))
        cnt["T"] += 1
        tok = ("T", cnt["T"])

        def f(eng, out=out, in_=in_, tok=tok):
            eng.transpose(out, in_, sb_id).then_inc(sem["T"], 1)
        ops["tensor"].append(f)
        return tok

    def _ev(ename, sname, kind, *args, waits=(), inc=False, drain=False, **kw):
        wait(ename, list(waits))
        tok = None
        if inc:
            cnt[sname] += 1
            tok = (sname, cnt[sname])

        def f(eng, kind=kind, args=args, kw=kw, tok=tok, drain=drain):
            if drain:
                eng.drain()
            i = getattr(eng, kind)(*args, **kw)
            if tok is not None:
                i.then_inc(sem[tok[0]], 1)
        ops[ename].append(f)
        return tok

    def vop(kind, *a, **kw):
        return _ev("vector", "V", kind, *a, **kw)

    def aop(kind, *a, **kw):
        return _ev("scalar", "A", kind, *a, **kw)

    def gop(kind, *a, **kw):
        return _ev("gpsimd", "G", kind, *a, **kw)

    def snap_all():
        return [(k, cnt[k]) for k in cnt]

    def vasnap():
        return [("V", cnt["V"]), ("A", cnt["A"])]

    def stat_finish(ncols, col0, eps, inv, waits, base=0):
        sl = lambda arr: sb_st[:, base + arr + col0: base + arr + col0 + ncols]
        vop("tensor_scalar", sl(ST_MEAN), sl(ST_SUM), inv, None, ALU.mult,
            waits=waits, drain=True)
        vop("tensor_scalar", sl(ST_M2), sl(ST_SQ), inv, float(eps), ALU.mult,
            ALU.add)
        vop("tensor_tensor", sl(ST_VAR), sl(ST_MEAN), sl(ST_MEAN), ALU.mult,
            drain=True)
        tv = vop("tensor_tensor", sl(ST_VAR), sl(ST_M2), sl(ST_VAR),
                 ALU.subtract, drain=True, inc=True)
        ta = aop("activation", sl(ST_STD), sl(ST_VAR), AF.Sqrt,
                 waits=[tv], inc=True)
        vop("reciprocal", sl(ST_RSTD), sl(ST_STD), waits=[ta], drain=True)
        tok = vop("scalar_tensor_tensor", sl(ST_NMR), sl(ST_MEAN), -1.0,
                  sl(ST_RSTD), ALU.mult, ALU.mult, drain=True, inc=True)
        return tok

    def rstd_ap(c, base=0):
        return sb_st[:, base + ST_RSTD + c: base + ST_RSTD + c + 1]

    def nmr_ap(c, base=0):
        return sb_st[:, base + ST_NMR + c: base + ST_NMR + c + 1]

    t_consts = [
        dma("sync", sb_id, ident_dr),
        dma("sync", sb_b1, bass.AP(b1_d, 0, [[1, 128], [128, FC]]), slow=True),
    ]

    out_dma_toks = []
    prev_seq_end = []

    for s in range(SPC):
        apply_toks = [None] * NB
        a1sq_toks = [None] * NB
        tile_last_tr = [None] * NB
        for bb in range(NB):
            par = bb % 2
            xt = xrebuf[:, par * 1024:(par + 1) * 1024]
            xnt = scratch[:, par * 1024:(par + 1) * 1024]
            w = list(t_consts) + list(prev_seq_end)
            if bb >= 2:
                w.append(apply_toks[bb - 2])
                w.append(a1sq_toks[bb - 2])
            src = bass.AP(x_in, s * S * D + bb * 128 * D, [[D, 128], [1, D]])
            tl = dma("sync", xt, src, waits=w)
            tsum = vop("reduce_sum", sb_st[:, ST_SUM + bb:ST_SUM + bb + 1], xt,
                       AX.X, waits=[tl] + list(prev_seq_end), inc=True)
            wsq = [tl] + list(prev_seq_end)
            if bb >= 2:
                wsq.append(tile_last_tr[bb - 2])
            tsq = aop("activation", xnt, xt, AF.Square,
                      accum_out=sb_st[:, ST_SQ + bb:ST_SQ + bb + 1],
                      waits=wsq, inc=True)
            a1sq_toks[bb] = tsq
            tfin = stat_finish(1, bb, LN_EPS, 1.0 / D, [tsum, tsq])
            tap = vop("tensor_scalar", xnt, xt, rstd_ap(bb), nmr_ap(bb),
                      ALU.mult, ALU.add, waits=[tfin], drain=True, inc=True)
            apply_toks[bb] = tap
            for c in range(DC):
                bk = (bb * DC + c) % 4
                ttr = transpose(bank(bk, 128), xnt[:, c * 128:(c + 1) * 128],
                                waits=[tap, bankc[bk]])
                dstA = arenaA[:, c * 2048 + bb * 128: c * 2048 + (bb + 1) * 128]
                if c % 2 == 0:
                    tev = vop("tensor_copy", dstA, bank(bk, 128), waits=[ttr],
                              inc=True)
                else:
                    tev = aop("copy", dstA, bank(bk, 128), waits=[ttr], inc=True)
                bankc[bk] = tev
                tile_last_tr[bb] = ttr
        xn1T_ready = vasnap()

        if _PHASES < 2:
            continue
        wg_toks = []
        for c in range(DC):
            src = bass.AP(wg_d, c * 128 * D, [[D, 128], [1, D]])
            wg_toks.append(dma("gpsimd", C_w8k[:, c * 1024:(c + 1) * 1024], src,
                               waits=list(prev_seq_end)))
        for t in range(NB):
            pr = t % 2
            bk0 = 4 + 2 * pr
            wts = xn1T_ready + wg_toks + [bankc[bk0], bankc[bk0 + 1]]
            for c in range(DC):
                for hh in range(2):
                    mm(bank(bk0 + hh),
                       arenaA[:, c * 2048 + t * 128: c * 2048 + (t + 1) * 128],
                       C_w8k[:, c * 1024 + hh * 512: c * 1024 + hh * 512 + 512],
                       start=(c == 0), stop=(c == DC - 1),
                       waits=wts if (c == 0 and hh == 0) else (),
                       inc=(c == DC - 1 and hh == 1))
            tmm = ("T", cnt["T"])
            tev = aop("activation", arenaB[:, t * 1024:(t + 1) * 1024],
                      bank(bk0, 1024), AF.Silu,
                      waits=[tmm] + list(prev_seq_end), inc=True)
            bankc[bk0] = tev
            bankc[bk0 + 1] = tev
        g_done_A = ("A", cnt["A"])

        if _PHASES < 3:
            continue
        prev_head_zmult = None
        sp_cons = [None, None]
        dt_prev_cons = [None, None]
        last_q_mm = None
        zmult_toks = []

        def load_wqkv(h, w):
            toks = []
            for wi, wd in enumerate((wq_d, wk_d, wv_d)):
                srcw = bass.AP(wd, h * 128, [[D, 128], [128 * D, DC], [1, 128]])
                dst = bass.AP(C_wqkv.tensor, C_wqkv.offset + wi * 1024,
                              [[14336, 128], [128, DC], [1, 128]])
                toks.append(dma("gpsimd", dst, srcw, waits=w))
            return toks

        wqkv_toks = load_wqkv(0, list(prev_seq_end))
        for h in range(H):
            wq_t, wk_t, wv_t = wqkv_toks

            def wslice(wi, c):
                off = wi * 1024 + c * 128
                return C_wqkv[:, off:off + 128]

            v_ev = []
            for mb in range(NB):
                pr = mb % 2
                wts = [wv_t, bankc[pr]] + xn1T_ready
                for c in range(DC):
                    mm(bank(pr, 128),
                       arenaA[:, c * 2048 + mb * 128: c * 2048 + (mb + 1) * 128],
                       wslice(2, c), start=(c == 0), stop=(c == DC - 1),
                       waits=wts if c == 0 else (), inc=(c == DC - 1))
                tmm = ("T", cnt["T"])
                tev = aop("copy", C_v[:, mb * 128:(mb + 1) * 128],
                          bank(pr, 128), waits=[tmm], inc=True)
                bankc[pr] = tev
                v_ev.append(tev)
            k_ev = []
            for ti in range(NT):
                pr = ti % 2
                wts = [wk_t, bankc[2 + pr]] + xn1T_ready
                for c in range(DC):
                    mm(bank(2 + pr), wslice(1, c),
                       arenaA[:, c * 2048 + ti * 512: c * 2048 + ti * 512 + 512],
                       start=(c == 0), stop=(c == DC - 1),
                       waits=wts if c == 0 else (), inc=(c == DC - 1))
                tmm = ("T", cnt["T"])
                tev = aop("copy", C_k[:, ti * 512:(ti + 1) * 512], bank(2 + pr),
                          waits=[tmm], inc=True)
                bankc[2 + pr] = tev
                k_ev.append(tev)
            q_ev = []
            for ti in range(NT):
                pr = ti % 2
                wts = [wq_t, bankc[2 + pr]] + xn1T_ready
                tq = None
                for c in range(DC):
                    tq = mm(bank(2 + pr), wslice(0, c),
                            arenaA[:, c * 2048 + ti * 512: c * 2048 + ti * 512 + 512],
                            start=(c == 0), stop=(c == DC - 1),
                            waits=wts if c == 0 else (), inc=(c == DC - 1))
                last_q_mm = tq
                tev = vop("tensor_scalar", C_q[:, ti * 512:(ti + 1) * 512],
                          bank(2 + pr), float(np.float32(1.0 / math.sqrt(DH))),
                          None, ALU.mult, waits=[tq], inc=True)
                bankc[2 + pr] = tev
                q_ev.append(tev)
            if h + 1 < H:
                wqkv_toks = load_wqkv(h + 1, [last_q_mm])

            gn_sum_toks = []
            SCB = (0, 1, 4)
            for nt in range(NT):
                nmb = 4 * nt + 4
                rbk = 2 + (nt % 2)
                sc_toks = [None] * nmb
                ev_toks = [None] * nmb
                av_toks = [None] * nmb
                dt_toks = [None] * nmb

                def emit_dtload(mb, nt=nt, ev_toks=ev_toks, dt_toks=dt_toks):
                    pr = mb % 2
                    ridx = 4 * nt - mb + 3
                    w = [ev_toks[mb - 2] if mb >= 2 else dt_prev_cons[pr]]
                    dt_toks[mb] = dma(
                        "sync", sb_dt[:, pr * 512:(pr + 1) * 512],
                        bass.AP(dt_d, (h * 16 + ridx) * 128 * 512,
                                [[512, 128], [1, 512]]), waits=w)

                def emit_scores(mb, nt=nt, sc_toks=sc_toks, ev_toks=ev_toks):
                    bk = SCB[mb % 3]
                    wts = [q_ev[nt], k_ev[mb // 4], bankc[bk]]
                    sc_toks[mb] = mm(bank(bk), C_k[:, mb * 128:(mb + 1) * 128],
                                     C_q[:, nt * 512: nt * 512 + 512],
                                     start=True, stop=True, waits=wts, inc=True)

                def emit_evac(mb, nt=nt, sc_toks=sc_toks, ev_toks=ev_toks,
                              dt_toks=dt_toks):
                    bk = SCB[mb % 3]
                    pr = mb % 2
                    wts = [sc_toks[mb], sp_cons[pr], dt_toks[mb]]
                    ev_toks[mb] = vop("tensor_tensor", C_Sp[pr], bank(bk),
                                      sb_dt[:, pr * 512:(pr + 1) * 512],
                                      ALU.mult, waits=wts, inc=True)
                    bankc[bk] = ev_toks[mb]
                    dt_prev_cons[pr] = ev_toks[mb]

                def emit_av(mb, nt=nt, nmb=nmb, rbk=rbk, ev_toks=ev_toks,
                            av_toks=av_toks):
                    pr = mb % 2
                    wts = [ev_toks[mb], v_ev[mb]]
                    if mb == 0:
                        wts += [bankc[rbk]]
                    tk = None
                    for j in range(4):
                        tk = mm(bank(rbk, 128, off=j * 128),
                                C_Sp[pr][:, j * 128:(j + 1) * 128],
                                C_v[:, mb * 128:(mb + 1) * 128],
                                start=(mb == 0 and j == 0),
                                stop=(mb == nmb - 1 and j == 3),
                                waits=wts if j == 0 else (), inc=(j == 3))
                    av_toks[mb] = tk
                    sp_cons[pr] = tk

                emit_dtload(0)
                emit_dtload(1) if nmb > 1 else None
                emit_scores(0)
                if nmb > 1:
                    emit_scores(1)
                for mb in range(nmb):
                    emit_evac(mb)
                    if mb + 2 < nmb:
                        emit_dtload(mb + 2)
                    if mb + 2 < nmb:
                        emit_scores(mb + 2)
                    emit_av(mb)
                t1last = None
                for j in range(4):
                    nbk = 4 * nt + j
                    yb = C_Y[:, nbk * 128:(nbk + 1) * 128]
                    t1 = aop("activation", yb, bank(rbk, 128, off=j * 128),
                             AF.Copy,
                             accum_out=sb_st[:, ST_SUM + nbk:ST_SUM + nbk + 1],
                             waits=[av_toks[nmb - 1], prev_head_zmult],
                             inc=True)
                    t2 = aop("activation", scratch[:, 0:128], yb, AF.Square,
                             accum_out=sb_st[:, ST_SQ + nbk:ST_SQ + nbk + 1],
                             drain=True, inc=True)
                    gn_sum_toks.append(t2)
                    t1last = t1
                bankc[rbk] = t1last
            tfin = stat_finish(NB, 0, GN_EPS, 1.0 / DH, gn_sum_toks)
            apv = apg = None
            for nbk in range(NB):
                yb = C_Y[:, nbk * 128:(nbk + 1) * 128]
                if nbk % 2 == 0:
                    apv = vop("tensor_scalar", yb, yb, rstd_ap(nbk),
                              nmr_ap(nbk), ALU.mult, ALU.add, waits=[tfin],
                              drain=(nbk == 0), inc=(nbk == NB - 2))
                else:
                    apg = gop("tensor_scalar", yb, yb, rstd_ap(nbk),
                              nmr_ap(nbk), ALU.mult, ALU.add, waits=[tfin],
                              drain=(nbk == 1), inc=(nbk == NB - 1))
            gslice = bass.AP(arenaB.tensor, arenaB.offset + h * 128,
                             [[16384, 128], [1024, NB], [1, 128]])
            yv = bass.AP(C_Y.tensor, C_Y.offset,
                         [[14336, 128], [128, NB], [1, 128]])
            prev_head_zmult = gop("tensor_tensor", gslice, gslice, yv, ALU.mult,
                                  waits=[apv, apg, g_done_A], drain=True,
                                  inc=True)
            zmult_toks.append(prev_head_zmult)

        if _PHASES < 4:
            continue
        head_T_done = ("T", cnt["T"])
        wo_toks = []
        for c in range(DC):
            src = bass.AP(wo_d, c * 128 * D, [[D, 128], [1, D]])
            wo_toks.append(dma("gpsimd", C_w8k[:, c * 1024:(c + 1) * 1024], src,
                               waits=[head_T_done] + vasnap()))
        zT_ev = {}
        for t in range(NB):
            for c in range(DC):
                bk = (t * DC + c) % 4
                ttr = transpose(
                    bank(bk, 128),
                    arenaB[:, t * 1024 + c * 128: t * 1024 + c * 128 + 128],
                    waits=[zmult_toks[-1], bankc[bk]])
                dstA = arenaA[:, c * 2048 + t * 128: c * 2048 + (t + 1) * 128]
                if c % 2 == 0:
                    tev = vop("tensor_copy", dstA, bank(bk, 128), waits=[ttr],
                              inc=True)
                else:
                    tev = aop("copy", dstA, bank(bk, 128), waits=[ttr], inc=True)
                bankc[bk] = tev
                zT_ev[t] = tev
        zT_ready = vasnap()

        x1_toks = [None] * NB
        sq_toks = [None] * NB
        for t in range(NB):
            pr = t % 2
            bk0 = 4 + 2 * pr
            wts = zT_ready + wo_toks + [bankc[bk0], bankc[bk0 + 1]]
            for c in range(DC):
                for hh in range(2):
                    mm(bank(bk0 + hh),
                       arenaA[:, c * 2048 + t * 128: c * 2048 + (t + 1) * 128],
                       C_w8k[:, c * 1024 + hh * 512: c * 1024 + hh * 512 + 512],
                       start=(c == 0), stop=(c == DC - 1),
                       waits=wts if (c == 0 and hh == 0) else (),
                       inc=(c == DC - 1 and hh == 1))
            tmm = ("T", cnt["T"])
            tsum = vop("reduce_sum", sb_st[:, ST_SUM + t:ST_SUM + t + 1],
                       bank(bk0, 1024), AX.X, waits=[tmm], inc=True)
            lnt = scratch[:, pr * 1024:(pr + 1) * 1024]
            wsq = [tmm, tsum]
            if t >= 2:
                wsq.append(x1_toks[t - 2])
            elif t < 2 and s >= 0:
                wsq.append(apply_toks[NB - 2 + t])
            tsq = aop("activation", lnt, bank(bk0, 1024), AF.Square,
                      accum_out=sb_st[:, ST_SQ + t:ST_SQ + t + 1], waits=wsq,
                      inc=True)
            tfin = stat_finish(1, t, LN_EPS, 1.0 / D, [tsum, tsq])
            tln = vop("tensor_scalar", lnt, bank(bk0, 1024), rstd_ap(t),
                      nmr_ap(t), ALU.mult, ALU.add, waits=[tfin], drain=True,
                      inc=True)
            bankc[bk0] = tln
            bankc[bk0 + 1] = tln
            xr = xrebuf[:, pr * 1024:(pr + 1) * 1024]
            wxr = [head_T_done]
            if t >= 2:
                wxr.append(x1_toks[t - 2])
                wxr.append(sq_toks[t - 2])
            else:
                wxr.append(apply_toks[NB - 2 + t])
                wxr.append(a1sq_toks[NB - 2 + t])
            txr = dma("gpsimd", xr,
                      bass.AP(x_in, s * S * D + t * 128 * D, [[D, 128], [1, D]]),
                      waits=wxr)
            x1_toks[t] = vop("scalar_tensor_tensor",
                             arenaB[:, t * 1024:(t + 1) * 1024], lnt, 1.0, xr,
                             ALU.mult, ALU.add,
                             accum_out=sb_st[:, XS + ST_SUM + t:XS + ST_SUM + t + 1],
                             waits=[tln, txr, zT_ev[t]], inc=True)
            sq_toks[t] = aop("activation", xr,
                             arenaB[:, t * 1024:(t + 1) * 1024], AF.Square,
                             accum_out=sb_st[:, XS + ST_SQ + t:XS + ST_SQ + t + 1],
                             waits=[x1_toks[t]], drain=True, inc=True)

        if _PHASES < 5:
            continue
        ffn_fin = stat_finish(NB, 0, LN_EPS, 1.0 / D,
                              [x1_toks[-1], sq_toks[-1]], base=XS)
        vop("tensor_copy", sb_st[:, ST_R2:ST_R2 + NB],
            sb_st[:, XS + ST_RSTD:XS + ST_RSTD + NB], waits=[ffn_fin],
            drain=True)
        r2n2_tok = vop("tensor_copy", sb_st[:, ST_N2:ST_N2 + NB],
                       sb_st[:, XS + ST_NMR:XS + ST_NMR + NB], inc=True)

        XS = 160
        ACCB = 8192
        grep = sb_dt[:, 0:1024]
        brep = xrebuf[:, 1024:2048]
        g_tok = dma("gpsimd", grep,
                    bass.AP(gam_d, 0, [[0, 128], [1, D]]),
                    waits=vasnap() + [("T", cnt["T"])])
        b_tok = dma("gpsimd", brep,
                    bass.AP(bet_d, 0, [[0, 128], [1, D]]),
                    waits=vasnap())
        rep_ready = None

        def tile_xn2T(o, tt, prev_ltr):
            slab2 = (o % 2) * 4096
            tglob = o * TPO + tt
            x1t = arenaB[:, tglob * 1024:(tglob + 1) * 1024]
            par = tt % 2
            xnt = scratch[:, par * 1024:(par + 1) * 1024]
            wap = [x1_toks[tglob], sq_toks[tglob], r2n2_tok, prev_ltr,
                   ("V", cnt["V"]), ("A", cnt["A"])]
            tap = vop("tensor_scalar", xnt, x1t, rstd_ap(tglob, base=XS),
                      nmr_ap(tglob, base=XS), ALU.mult, ALU.add,
                      waits=wap, drain=True, inc=True)
            ltr = None
            for c in range(DC):
                bk = 2 + (tt * DC + c) % 2
                ltr = transpose(bank(bk, 128), xnt[:, c * 128:(c + 1) * 128],
                                waits=[tap, bankc[bk]])
                dstA = arenaA[:, slab2 + c * OUTER + tt * 128:
                              slab2 + c * OUTER + (tt + 1) * 128]
                if c % 2 == 0:
                    tev = vop("tensor_copy", dstA, bank(bk, 128), waits=[ltr],
                              inc=True)
                else:
                    tev = aop("copy", dstA, bank(bk, 128), waits=[ltr],
                              inc=True)
                bankc[bk] = tev
            return ltr

        def tail_tile(o, tt, acc_toks, xn2T_tr):
            nonlocal t3_prev, t8_prev
            tglob = o * TPO + tt
            sc = (o % 2) * 4
            zc = sc + tt
            x1t = arenaB[:, tglob * 1024:(tglob + 1) * 1024]
            accb = 8192 + (o % 2) * 4096
            acct = arenaA[:, accb + tt * 1024: accb + (tt + 1) * 1024]
            zt = scratch[:, 0:1024]
            ot = xrebuf[:, 0:1024]
            t1 = vop("tensor_scalar", zt, x1t,
                     rstd_ap(tglob, base=XS), nmr_ap(tglob, base=XS),
                     ALU.mult, ALU.add,
                     waits=list(acc_toks) + [g_tok, b_tok, t3_prev, xn2T_tr],
                     drain=True, inc=True)
            t2 = vop("scalar_tensor_tensor", zt, zt, 1.0, acct,
                     ALU.mult, ALU.add,
                     accum_out=sb_st[:, ST_SUM + zc:ST_SUM + zc + 1],
                     waits=[t1], drain=True, inc=True)
            t3_prev = t2
            wdump = [t2, t8_prev]
            if out_dma_toks:
                wdump.append(out_dma_toks[-1])
            t4 = aop("activation", ot, zt, AF.Square,
                     accum_out=sb_st[:, ST_SQ + zc:ST_SQ + zc + 1],
                     waits=wdump, drain=True, inc=True)
            t5 = stat_finish(1, zc, LN_EPS, 1.0 / D, [t2, t4])
            t6 = vop("tensor_scalar", zt, zt, rstd_ap(zc), nmr_ap(zc),
                     ALU.mult, ALU.add, waits=[t5, t8_prev, t4], drain=True,
                     inc=True)
            t7 = gop("tensor_tensor", zt, zt, grep, ALU.mult, waits=[t6],
                     drain=True, inc=True)
            t8 = vop("scalar_tensor_tensor", zt, zt, 1.0, brep,
                     ALU.mult, ALU.add,
                     accum_out=sb_st[:, ST_SUM + 8 + zc:ST_SUM + 9 + zc],
                     waits=[t7], drain=True, inc=True)
            t8_prev = t8
            t10 = aop("activation", ot, zt, AF.Square,
                      accum_out=sb_st[:, ST_SQ + 8 + zc:ST_SQ + 9 + zc],
                      waits=[t8, t4], drain=True, inc=True)
            t11 = stat_finish(1, 8 + zc, LN_EPS, 1.0 / D, [t8, t10])
            t12 = vop("tensor_scalar", ot, zt, rstd_ap(8 + zc), nmr_ap(8 + zc),
                      ALU.mult, ALU.add, waits=[t11, t10], drain=True, inc=True)
            t13 = vop("tensor_tensor", ot, ot, x1t, ALU.add, waits=[t12],
                      drain=True, inc=True)
            dst = bass.AP(y_out, s * S * D + (o * OUTER + tt * 128) * D,
                          [[D, 128], [1, D]])
            out_dma_toks.append(dma("gpsimd", dst, ot, waits=[t13],
                                    semname="O"))

        t3_prev = None
        t8_prev = None
        prev_ltr = None
        for tt0 in range(TPO):
            prev_ltr = tile_xn2T(0, tt0, prev_ltr)
        next_tr, next_ready = prev_ltr, vasnap()
        acc_last_of = {}
        w1_tok = [None, None]
        w1_free = [None, None]
        w2_free = [None] * GRP
        hslot_free = [None] * GRP
        for o in range(NOUT):
            slab = (o % 2) * 4096
            xn2T_ready = next_ready
            accb_o = 8192 + (o % 2) * 4096

            acc_last = None
            last_consume_T = None
            for g in range(NGRP):
                h_ready = [None] * GRP
                for fi in range(2):
                    fglob = g * GRP + fi
                    w1_tok[fi % 2] = dma(
                        "sync", C_w1b[fi % 2],
                        bass.AP(w1_d, fglob * 128,
                                [[F, 128], [128 * F, DC], [1, 128]]),
                        waits=[w1_free[fi % 2]])
                for fi in range(GRP):
                    fglob = g * GRP + fi
                    wp = fi % 2
                    bk0 = fi % 2
                    wts = xn2T_ready + [w1_tok[wp], bankc[bk0]]
                    tmm_ = None
                    for c in range(DC):
                        tmm_ = mm(bank(bk0),
                                  C_w1b[wp][:, c * 128:(c + 1) * 128],
                                  arenaA[:, slab + c * OUTER:
                                         slab + c * OUTER + OUTER],
                                  start=(c == 0), stop=(c == DC - 1),
                                  waits=wts if c == 0 else (),
                                  inc=(c == DC - 1))
                    w1_free[wp] = tmm_
                    if fi + 2 < GRP:
                        w1_tok[wp] = dma(
                            "sync", C_w1b[wp],
                            bass.AP(w1_d, (fglob + 2) * 128,
                                    [[F, 128], [128 * F, DC], [1, 128]]),
                            waits=[tmm_])
                    trl = aop("activation",
                              C_hg[:, fi * OUTER:(fi + 1) * OUTER],
                              bank(bk0, OUTER), AF.Relu,
                              bias=sb_b1[:, fglob:fglob + 1],
                              waits=[tmm_, hslot_free[fi]], inc=True)
                    bankc[bk0] = trl
                    h_ready[fi] = trl
                for fi in range(GRP):
                    fglob = g * GRP + fi
                    tw2 = dma("sync", C_w2g[:, fi * 1024:(fi + 1) * 1024],
                              bass.AP(w2_d, fglob * 128 * D, [[D, 128], [1, D]]),
                              waits=[w2_free[fi]])
                    w2_free[fi] = tw2
                for hh in range(2):
                    for ns in range(OUTER // 256):
                        pp = ns % 2
                        bk0 = 4 + 2 * pp
                        wts = [bankc[bk0], bankc[bk0 + 1]] + h_ready + \
                            [w2_free[x] for x in range(GRP)]
                        tmm_ = None
                        for fi in range(GRP):
                            for nb2 in range(2):
                                tmm_ = mm(
                                    bank(bk0 + nb2),
                                    C_hg[:, fi * OUTER + (ns * 2 + nb2) * 128:
                                         fi * OUTER + (ns * 2 + nb2) * 128 + 128],
                                    C_w2g[:, fi * 1024 + hh * 512:
                                          fi * 1024 + hh * 512 + 512],
                                    start=(fi == 0), stop=(fi == GRP - 1),
                                    waits=wts if (fi == 0 and nb2 == 0) else (),
                                    inc=(fi == GRP - 1 and nb2 == 1))
                        last_consume_T = tmm_
                        for nb2 in range(2):
                            base = accb_o + (ns * 2 + nb2) * 1024 + hh * 512
                            dst = arenaA[:, base: base + 512]
                            if g == 0:
                                ev = vop("tensor_copy", dst, bank(bk0 + nb2),
                                         waits=[last_consume_T], inc=True)
                            else:
                                ev = vop("tensor_tensor", dst, dst,
                                         bank(bk0 + nb2), ALU.add,
                                         waits=[last_consume_T], inc=True)
                            acc_last = ev
                            bankc[bk0 + nb2] = ev
                for fi in range(GRP):
                    hslot_free[fi] = last_consume_T
                    w2_free[fi] = last_consume_T
                if o + 1 < NOUT:
                    next_tr = tile_xn2T(o + 1, g, next_tr)
                if o >= 1:
                    tail_tile(o - 1, g, acc_last_of[o - 1], next_tr)
            atoks = []
            for tt2 in range(TPO):
                atoks.append(dma(
                    "gpsimd",
                    bass.AP(arenaA.tensor, accb_o + tt2 * 1024,
                            [[16384, 128], [1, 1024]]),
                    bass.AP(b2_d, 0, [[0, 128], [1, 1024]]),
                    waits=[acc_last if tt2 == 0 else None], accum=True))
            acc_last_of[o] = atoks
            if o + 1 < NOUT:
                next_ready = vasnap()
        for tt in range(TPO):
            tail_tile(NOUT - 1, tt, acc_last_of[NOUT - 1], next_tr)
        prev_seq_end = snap_all()

    if _PHASES < 5:
        t_end = snap_all()
        for s2 in range(SPC):
            for t2 in range(NB):
                d2 = bass.AP(y_out, s2 * S * D + t2 * 128 * D, [[D, 128], [1, D]])
                dma("gpsimd", d2, arenaB[:, t2 * 1024:(t2 + 1) * 1024],
                    waits=t_end, semname="O")
    fin_counts = [(k, cnt[k]) for k in OL]

    def fin(eng, fc=tuple(fin_counts)):
        for k, n in fc:
            eng.wait_ge(sem[k], n)
    ops["gpsimd"].append(fin)

    import contextlib
    with contextlib.ExitStack() as stack:
        for k in list(cnt.keys()):
            sem[k] = stack.enter_context(nc.semaphore(k))
        block = stack.enter_context(nc.Block())

        @block.sync
        def _(eng):
            for f in ops["sync"]:
                f(eng)

        @block.gpsimd
        def _(eng):
            for f in ops["gpsimd"]:
                f(eng)

        @block.tensor
        def _(eng):
            for f in ops["tensor"]:
                f(eng)

        @block.vector
        def _(eng):
            for f in ops["vector"]:
                f(eng)

        @block.scalar
        def _(eng):
            for f in ops["scalar"]:
                f(eng)

    return nc


def _decay_tiles():
    import jax
    import jax.numpy as jnp
    try:
        cpu = jax.devices("cpu")[0]
        with jax.default_device(cpu):
            gammas = 1.0 - jnp.exp2(-5.0 - jnp.arange(H, dtype=jnp.float32))
            log_g = jnp.log(gammas)
            diffs = jnp.arange(S, dtype=jnp.float32)
            dv = np.asarray(jax.device_get(
                jnp.exp(log_g[:, None] * diffs[None, :])), dtype=np.float32)
    except Exception:
        g32 = (1.0 - np.exp2(-5.0 - np.arange(H, dtype=np.float32))).astype(
            np.float32)
        lg = np.log(g32).astype(np.float32)
        dfs = np.arange(S, dtype=np.float32)
        dv = np.exp(lg[:, None] * dfs[None, :]).astype(np.float32)
    tiles = np.zeros((H, 16, 128, 512), dtype=np.float32)
    ml = np.arange(128)[:, None]
    nl = np.arange(512)[None, :]
    for ridx in range(16):
        r = ridx * 128 - 384
        d = r + nl - ml
        valid = d >= 0
        dcl = np.clip(d, 0, S - 1)
        for h in range(H):
            tiles[h, ridx] = np.where(valid, dv[h][dcl], 0.0)
    return tiles


def kernel(**inputs):
    x = np.ascontiguousarray(np.asarray(inputs["L0_attn_X"], dtype=np.float32))
    wq = np.ascontiguousarray(np.asarray(inputs["L0_attn_Wq"], dtype=np.float32))
    wk = np.ascontiguousarray(np.asarray(inputs["L0_attn_Wk"], dtype=np.float32))
    wv = np.ascontiguousarray(np.asarray(inputs["L0_attn_Wv"], dtype=np.float32))
    wg = np.ascontiguousarray(np.asarray(inputs["L0_attn_W_G"], dtype=np.float32))
    wo = np.ascontiguousarray(np.asarray(inputs["L0_attn_W_O"], dtype=np.float32))
    w1 = np.ascontiguousarray(np.asarray(inputs["L0_ffn_w1"][0], dtype=np.float32))
    b1 = np.ascontiguousarray(np.asarray(inputs["L0_ffn_b1"][0], dtype=np.float32))
    w2 = np.ascontiguousarray(np.asarray(inputs["L0_ffn_w2"][0], dtype=np.float32))
    b2 = np.ascontiguousarray(np.asarray(inputs["L0_ffn_b2"][0], dtype=np.float32))
    gam = np.ascontiguousarray(np.asarray(inputs["L0_ffn_gamma"], dtype=np.float32))
    bet = np.ascontiguousarray(np.asarray(inputs["L0_ffn_beta"], dtype=np.float32))

    nc = build_program()
    dt = _decay_tiles()
    in_maps = []
    for c in range(NCORES):
        in_maps.append({
            "x": np.ascontiguousarray(x[c * SPC:(c + 1) * SPC]),
            "wq": wq, "wk": wk, "wv": wv, "wg": wg, "wo": wo,
            "w1": w1, "b1": b1, "w2": w2, "b2": b2, "gam": gam, "bet": bet,
            "dt": dt,
        })
    res = run_bass_kernel_spmd(nc, in_maps, core_ids=list(range(NCORES)))
    out = np.concatenate([r["y"] for r in res.results], axis=0)
    return out.astype(np.float32)


# version 21
# speedup vs baseline: 1.0799x; 1.0011x over previous
import math
import numpy as np
import concourse.bass as bass
import concourse.mybir as mybir
from concourse.bass_utils import run_bass_kernel_spmd

F32 = mybir.dt.float32
BF16 = mybir.dt.bfloat16
AF = mybir.ActivationFunctionType
ALU = mybir.AluOpType
AX = mybir.AxisListType

B, S, D, H, DH, F = 16, 2048, 1024, 8, 128, 4096
NCORES = 8
SPC = B // NCORES
NB = S // 128
NT = S // 512
DC = D // 128
FC = F // 128
OUTER = 512
NOUT = S // OUTER
TPO = OUTER // 128
GRP = 8
NGRP = FC // GRP
LN_EPS = 1e-5
GN_EPS = 1e-6

ENGS = ("sync", "gpsimd", "tensor", "vector", "scalar")
_PHASES = 99


def build_program():
    nc = bass.Bass("TRN2", target_bir_lowering=False)

    x_in = nc.dram_tensor("x", [SPC, S, D], F32, kind="ExternalInput")
    wq_d = nc.dram_tensor("wq", [D, D], F32, kind="ExternalInput")
    wk_d = nc.dram_tensor("wk", [D, D], F32, kind="ExternalInput")
    wv_d = nc.dram_tensor("wv", [D, D], F32, kind="ExternalInput")
    wg_d = nc.dram_tensor("wg", [D, D], F32, kind="ExternalInput")
    wo_d = nc.dram_tensor("wo", [D, D], F32, kind="ExternalInput")
    w1_d = nc.dram_tensor("w1", [D, F], F32, kind="ExternalInput")
    b1_d = nc.dram_tensor("b1", [F], F32, kind="ExternalInput")
    w2_d = nc.dram_tensor("w2", [F, D], F32, kind="ExternalInput")
    b2_d = nc.dram_tensor("b2", [D], F32, kind="ExternalInput")
    gam_d = nc.dram_tensor("gam", [D], F32, kind="ExternalInput")
    bet_d = nc.dram_tensor("bet", [D], F32, kind="ExternalInput")
    y_out = nc.dram_tensor("y", [SPC, S, D], F32, kind="ExternalOutput")

    ident_dr = nc.inline_tensor(np.eye(128, dtype=np.float32), "ident").ap()
    dt_d = nc.dram_tensor("dt", [H, 16, 128, 512], F32, kind="ExternalInput")

    arenaA = nc.alloc_sbuf_tensor("arenaA", [128, 16384], F32).ap()
    arenaB = nc.alloc_sbuf_tensor("arenaB", [128, 16384], F32).ap()
    arenaC = nc.alloc_sbuf_tensor("arenaC", [128, 14336], F32).ap()
    sb_dt = nc.alloc_sbuf_tensor("sb_dt", [128, 1024], F32).ap()
    sb_id = nc.alloc_sbuf_tensor("sb_id", [128, 128], F32).ap()
    sb_b1 = nc.alloc_sbuf_tensor("sb_b1", [128, FC], F32).ap()
    sb_st = nc.alloc_sbuf_tensor("sb_st", [128, 320], F32).ap()
    scratch = nc.alloc_sbuf_tensor("scratch", [128, 2048], F32).ap()
    xrebuf = nc.alloc_sbuf_tensor("xrebuf", [128, 2048], F32).ap()
    PS = nc.alloc_psum_tensor("PS", [128, 4096], F32).ap()

    def bank(i, w=512, off=0):
        return PS[:, i * 512 + off: i * 512 + off + w]

    C_q = arenaC[:, 0:2048]
    C_k = arenaC[:, 2048:4096]
    C_v = arenaC[:, 4096:6144]
    C_Y = arenaC[:, 6144:8192]
    C_Sp = [arenaC[:, 8192 + i * 512: 8192 + (i + 1) * 512] for i in range(2)]
    C_wqkv = arenaC[:, 9216:12288]
    C_w8k = arenaC[:, 0:8192]
    C_hg = arenaC[:, 0:4096]
    C_w2g = arenaC[:, 4096:12288]
    C_w1b = [arenaC[:, 12288 + i * 1024: 12288 + (i + 1) * 1024] for i in range(2)]

    ST_SUM, ST_SQ, ST_MEAN, ST_M2, ST_VAR, ST_STD, ST_RSTD, ST_NMR = \
        0, 16, 32, 48, 64, 80, 96, 112
    ST_R2, ST_N2 = 128, 144
    XS = 160

    ops = {e: [] for e in ENGS}
    DL = [f"D{i}" for i in range(8)]
    EL = ["E0", "E1"]
    OL = ["O0", "O1"]
    cnt = {k: 0 for k in (DL + EL + OL + ["T", "V", "A", "G"])}
    lane_idx = {"sync": 0, "gpsimd": 0, "out": 0}
    last_wait = {e: {} for e in ENGS}
    sem = {}
    bankc = {i: None for i in range(8)}

    def wait(ename, waits):
        lw = last_wait[ename]
        todo = []
        for t in waits:
            if t is None:
                continue
            sname, v = t
            if v <= 0 or lw.get(sname, 0) >= v:
                continue
            lw[sname] = v
            todo.append((sname, v))
        if todo:
            def f(eng, todo=tuple(todo)):
                for sname, v in todo:
                    eng.wait_ge(sem[sname], v)
            ops[ename].append(f)

    def dma(ename, dst, src, waits=(), semname=None, slow=False):
        if semname == "O":
            sm = OL[lane_idx["out"] % len(OL)]
            lane_idx["out"] += 1
        elif ename == "sync":
            sm = DL[lane_idx["sync"] % len(DL)]
            lane_idx["sync"] += 1
        else:
            sm = EL[lane_idx["gpsimd"] % len(EL)]
            lane_idx["gpsimd"] += 1
        wait(ename, list(waits) + [(sm, cnt[sm])])
        cnt[sm] += 16
        tok = (sm, cnt[sm])

        def f(eng, dst=dst, src=src, sm=sm, slow=slow):
            eng.dma_start(dst, src,
                          allow_slow_non_contiguous=slow).then_inc(sem[sm], 16)
        ops[ename].append(f)
        return tok

    def mm(out, lhsT, rhs, start, stop, waits=(), inc=False):
        wait("tensor", list(waits))
        tok = None
        if inc:
            cnt["T"] += 1
            tok = ("T", cnt["T"])

        def f(eng, out=out, lhsT=lhsT, rhs=rhs, start=start, stop=stop, tok=tok):
            i = eng.matmul(out, lhsT, rhs, start=start, stop=stop)
            if tok is not None:
                i.then_inc(sem["T"], 1)
        ops["tensor"].append(f)
        return tok

    def transpose(out, in_, waits=(), inc=True):
        wait("tensor", list(waits))
        cnt["T"] += 1
        tok = ("T", cnt["T"])

        def f(eng, out=out, in_=in_, tok=tok):
            eng.transpose(out, in_, sb_id).then_inc(sem["T"], 1)
        ops["tensor"].append(f)
        return tok

    def _ev(ename, sname, kind, *args, waits=(), inc=False, drain=False, **kw):
        wait(ename, list(waits))
        tok = None
        if inc:
            cnt[sname] += 1
            tok = (sname, cnt[sname])

        def f(eng, kind=kind, args=args, kw=kw, tok=tok, drain=drain):
            if drain:
                eng.drain()
            i = getattr(eng, kind)(*args, **kw)
            if tok is not None:
                i.then_inc(sem[tok[0]], 1)
        ops[ename].append(f)
        return tok

    def vop(kind, *a, **kw):
        return _ev("vector", "V", kind, *a, **kw)

    def aop(kind, *a, **kw):
        return _ev("scalar", "A", kind, *a, **kw)

    def gop(kind, *a, **kw):
        return _ev("gpsimd", "G", kind, *a, **kw)

    def snap_all():
        return [(k, cnt[k]) for k in cnt]

    def vasnap():
        return [("V", cnt["V"]), ("A", cnt["A"])]

    def stat_finish(ncols, col0, eps, inv, waits, base=0):
        sl = lambda arr: sb_st[:, base + arr + col0: base + arr + col0 + ncols]
        vop("tensor_scalar", sl(ST_MEAN), sl(ST_SUM), inv, None, ALU.mult,
            waits=waits, drain=True)
        vop("tensor_scalar", sl(ST_M2), sl(ST_SQ), inv, float(eps), ALU.mult,
            ALU.add)
        vop("tensor_tensor", sl(ST_VAR), sl(ST_MEAN), sl(ST_MEAN), ALU.mult,
            drain=True)
        tv = vop("tensor_tensor", sl(ST_VAR), sl(ST_M2), sl(ST_VAR),
                 ALU.subtract, drain=True, inc=True)
        ta = aop("activation", sl(ST_STD), sl(ST_VAR), AF.Sqrt,
                 waits=[tv], inc=True)
        vop("reciprocal", sl(ST_RSTD), sl(ST_STD), waits=[ta], drain=True)
        tok = vop("scalar_tensor_tensor", sl(ST_NMR), sl(ST_MEAN), -1.0,
                  sl(ST_RSTD), ALU.mult, ALU.mult, drain=True, inc=True)
        return tok

    def rstd_ap(c, base=0):
        return sb_st[:, base + ST_RSTD + c: base + ST_RSTD + c + 1]

    def nmr_ap(c, base=0):
        return sb_st[:, base + ST_NMR + c: base + ST_NMR + c + 1]

    t_consts = [
        dma("sync", sb_id, ident_dr),
        dma("sync", sb_b1, bass.AP(b1_d, 0, [[1, 128], [128, FC]]), slow=True),
    ]

    out_dma_toks = []
    odma_slot = [None, None]
    prev_seq_end = []

    for s in range(SPC):
        apply_toks = [None] * NB
        tile_last_tr = [None] * NB
        for bb in range(NB):
            par = bb % 2
            xt = xrebuf[:, par * 1024:(par + 1) * 1024]
            xnt = scratch[:, par * 1024:(par + 1) * 1024]
            w = list(t_consts) + list(prev_seq_end)
            if bb >= 2:
                w.append(apply_toks[bb - 2])
            src = bass.AP(x_in, s * S * D + bb * 128 * D, [[D, 128], [1, D]])
            tl = dma("sync", xt, src, waits=w)
            tsum = vop("reduce_sum", sb_st[:, ST_SUM + bb:ST_SUM + bb + 1], xt,
                       AX.X, waits=[tl] + list(prev_seq_end), inc=True)
            wsq = [tl] + list(prev_seq_end)
            if bb >= 2:
                wsq.append(tile_last_tr[bb - 2])
            tsq = aop("activation", xnt, xt, AF.Square,
                      accum_out=sb_st[:, ST_SQ + bb:ST_SQ + bb + 1],
                      waits=wsq, inc=True)
            tfin = stat_finish(1, bb, LN_EPS, 1.0 / D, [tsum, tsq])
            tap = vop("tensor_scalar", xnt, xt, rstd_ap(bb), nmr_ap(bb),
                      ALU.mult, ALU.add, waits=[tfin], drain=True, inc=True)
            apply_toks[bb] = tap
            for c in range(DC):
                bk = (bb * DC + c) % 4
                ttr = transpose(bank(bk, 128), xnt[:, c * 128:(c + 1) * 128],
                                waits=[tap, bankc[bk]])
                dstA = arenaA[:, c * 2048 + bb * 128: c * 2048 + (bb + 1) * 128]
                if c % 2 == 0:
                    tev = vop("tensor_copy", dstA, bank(bk, 128), waits=[ttr],
                              inc=True)
                else:
                    tev = aop("copy", dstA, bank(bk, 128), waits=[ttr], inc=True)
                bankc[bk] = tev
                tile_last_tr[bb] = ttr
        xn1T_ready = vasnap()

        if _PHASES < 2:
            continue
        wg_toks = []
        for c in range(DC):
            src = bass.AP(wg_d, c * 128 * D, [[D, 128], [1, D]])
            wg_toks.append(dma("gpsimd", C_w8k[:, c * 1024:(c + 1) * 1024], src,
                               waits=list(prev_seq_end)))
        for t in range(NB):
            pr = t % 2
            bk0 = 4 + 2 * pr
            wts = xn1T_ready + wg_toks + [bankc[bk0], bankc[bk0 + 1]]
            for c in range(DC):
                for hh in range(2):
                    mm(bank(bk0 + hh),
                       arenaA[:, c * 2048 + t * 128: c * 2048 + (t + 1) * 128],
                       C_w8k[:, c * 1024 + hh * 512: c * 1024 + hh * 512 + 512],
                       start=(c == 0), stop=(c == DC - 1),
                       waits=wts if (c == 0 and hh == 0) else (),
                       inc=(c == DC - 1 and hh == 1))
            tmm = ("T", cnt["T"])
            tev = aop("activation", arenaB[:, t * 1024:(t + 1) * 1024],
                      bank(bk0, 1024), AF.Silu,
                      waits=[tmm] + list(prev_seq_end), inc=True)
            bankc[bk0] = tev
            bankc[bk0 + 1] = tev
        g_done_A = ("A", cnt["A"])

        if _PHASES < 3:
            continue
        prev_head_zmult = None
        sp_cons = [None, None]
        dt_prev_cons = [None, None]
        last_q_mm = None
        zmult_toks = []

        def load_wqkv(h, w):
            toks = []
            for wi, wd in enumerate((wq_d, wk_d, wv_d)):
                srcw = bass.AP(wd, h * 128, [[D, 128], [128 * D, DC], [1, 128]])
                dst = bass.AP(C_wqkv.tensor, C_wqkv.offset + wi * 1024,
                              [[14336, 128], [128, DC], [1, 128]])
                toks.append(dma("gpsimd", dst, srcw, waits=w))
            return toks

        wqkv_toks = load_wqkv(0, list(prev_seq_end))
        for h in range(H):
            wq_t, wk_t, wv_t = wqkv_toks

            def wslice(wi, c):
                off = wi * 1024 + c * 128
                return C_wqkv[:, off:off + 128]

            v_ev = []
            for mb in range(NB):
                pr = mb % 2
                wts = [wv_t, bankc[pr]] + xn1T_ready
                for c in range(DC):
                    mm(bank(pr, 128),
                       arenaA[:, c * 2048 + mb * 128: c * 2048 + (mb + 1) * 128],
                       wslice(2, c), start=(c == 0), stop=(c == DC - 1),
                       waits=wts if c == 0 else (), inc=(c == DC - 1))
                tmm = ("T", cnt["T"])
                tev = aop("copy", C_v[:, mb * 128:(mb + 1) * 128],
                          bank(pr, 128), waits=[tmm], inc=True)
                bankc[pr] = tev
                v_ev.append(tev)
            k_ev = []
            for ti in range(NT):
                pr = ti % 2
                wts = [wk_t, bankc[2 + pr]] + xn1T_ready
                for c in range(DC):
                    mm(bank(2 + pr), wslice(1, c),
                       arenaA[:, c * 2048 + ti * 512: c * 2048 + ti * 512 + 512],
                       start=(c == 0), stop=(c == DC - 1),
                       waits=wts if c == 0 else (), inc=(c == DC - 1))
                tmm = ("T", cnt["T"])
                tev = aop("copy", C_k[:, ti * 512:(ti + 1) * 512], bank(2 + pr),
                          waits=[tmm], inc=True)
                bankc[2 + pr] = tev
                k_ev.append(tev)
            q_ev = []
            for ti in range(NT):
                pr = ti % 2
                wts = [wq_t, bankc[2 + pr]] + xn1T_ready
                tq = None
                for c in range(DC):
                    tq = mm(bank(2 + pr), wslice(0, c),
                            arenaA[:, c * 2048 + ti * 512: c * 2048 + ti * 512 + 512],
                            start=(c == 0), stop=(c == DC - 1),
                            waits=wts if c == 0 else (), inc=(c == DC - 1))
                last_q_mm = tq
                tev = vop("tensor_scalar", C_q[:, ti * 512:(ti + 1) * 512],
                          bank(2 + pr), float(np.float32(1.0 / math.sqrt(DH))),
                          None, ALU.mult, waits=[tq], inc=True)
                bankc[2 + pr] = tev
                q_ev.append(tev)
            if h + 1 < H:
                wqkv_toks = load_wqkv(h + 1, [last_q_mm])

            gn_sum_toks = []
            SCB = (0, 1, 4)
            for nt in range(NT):
                nmb = 4 * nt + 4
                rbk = 2 + (nt % 2)
                sc_toks = [None] * nmb
                ev_toks = [None] * nmb
                av_toks = [None] * nmb
                dt_toks = [None] * nmb

                def emit_dtload(mb, nt=nt, ev_toks=ev_toks, dt_toks=dt_toks):
                    pr = mb % 2
                    ridx = 4 * nt - mb + 3
                    w = [ev_toks[mb - 2] if mb >= 2 else dt_prev_cons[pr]]
                    dt_toks[mb] = dma(
                        "sync", sb_dt[:, pr * 512:(pr + 1) * 512],
                        bass.AP(dt_d, (h * 16 + ridx) * 128 * 512,
                                [[512, 128], [1, 512]]), waits=w)

                def emit_scores(mb, nt=nt, sc_toks=sc_toks, ev_toks=ev_toks):
                    bk = SCB[mb % 3]
                    wts = [q_ev[nt], k_ev[mb // 4], bankc[bk]]
                    sc_toks[mb] = mm(bank(bk), C_k[:, mb * 128:(mb + 1) * 128],
                                     C_q[:, nt * 512: nt * 512 + 512],
                                     start=True, stop=True, waits=wts, inc=True)

                def emit_evac(mb, nt=nt, sc_toks=sc_toks, ev_toks=ev_toks,
                              dt_toks=dt_toks):
                    bk = SCB[mb % 3]
                    pr = mb % 2
                    wts = [sc_toks[mb], sp_cons[pr], dt_toks[mb]]
                    ev_toks[mb] = vop("tensor_tensor", C_Sp[pr], bank(bk),
                                      sb_dt[:, pr * 512:(pr + 1) * 512],
                                      ALU.mult, waits=wts, inc=True)
                    bankc[bk] = ev_toks[mb]
                    dt_prev_cons[pr] = ev_toks[mb]

                def emit_av(mb, nt=nt, nmb=nmb, rbk=rbk, ev_toks=ev_toks,
                            av_toks=av_toks):
                    pr = mb % 2
                    wts = [ev_toks[mb], v_ev[mb]]
                    if mb == 0:
                        wts += [bankc[rbk]]
                    tk = None
                    for j in range(4):
                        tk = mm(bank(rbk, 128, off=j * 128),
                                C_Sp[pr][:, j * 128:(j + 1) * 128],
                                C_v[:, mb * 128:(mb + 1) * 128],
                                start=(mb == 0 and j == 0),
                                stop=(mb == nmb - 1 and j == 3),
                                waits=wts if j == 0 else (), inc=(j == 3))
                    av_toks[mb] = tk
                    sp_cons[pr] = tk

                emit_dtload(0)
                emit_dtload(1) if nmb > 1 else None
                emit_scores(0)
                if nmb > 1:
                    emit_scores(1)
                for mb in range(nmb):
                    emit_evac(mb)
                    if mb + 2 < nmb:
                        emit_dtload(mb + 2)
                    if mb + 2 < nmb:
                        emit_scores(mb + 2)
                    emit_av(mb)
                t1last = None
                for j in range(4):
                    nbk = 4 * nt + j
                    yb = C_Y[:, nbk * 128:(nbk + 1) * 128]
                    t1 = aop("activation", yb, bank(rbk, 128, off=j * 128),
                             AF.Copy,
                             accum_out=sb_st[:, ST_SUM + nbk:ST_SUM + nbk + 1],
                             waits=[av_toks[nmb - 1], prev_head_zmult],
                             inc=True)
                    t2 = aop("activation", scratch[:, 0:128], yb, AF.Square,
                             accum_out=sb_st[:, ST_SQ + nbk:ST_SQ + nbk + 1],
                             drain=True, inc=True)
                    gn_sum_toks.append(t2)
                    t1last = t1
                bankc[rbk] = t1last
            tfin = stat_finish(NB, 0, GN_EPS, 1.0 / DH, gn_sum_toks)
            apv = apg = None
            for nbk in range(NB):
                yb = C_Y[:, nbk * 128:(nbk + 1) * 128]
                if nbk % 2 == 0:
                    apv = vop("tensor_scalar", yb, yb, rstd_ap(nbk),
                              nmr_ap(nbk), ALU.mult, ALU.add, waits=[tfin],
                              drain=(nbk == 0), inc=(nbk == NB - 2))
                else:
                    apg = gop("tensor_scalar", yb, yb, rstd_ap(nbk),
                              nmr_ap(nbk), ALU.mult, ALU.add, waits=[tfin],
                              drain=(nbk == 1), inc=(nbk == NB - 1))
            gslice = bass.AP(arenaB.tensor, arenaB.offset + h * 128,
                             [[16384, 128], [1024, NB], [1, 128]])
            yv = bass.AP(C_Y.tensor, C_Y.offset,
                         [[14336, 128], [128, NB], [1, 128]])
            prev_head_zmult = gop("tensor_tensor", gslice, gslice, yv, ALU.mult,
                                  waits=[apv, apg, g_done_A], drain=True,
                                  inc=True)
            zmult_toks.append(prev_head_zmult)

        if _PHASES < 4:
            continue
        head_T_done = ("T", cnt["T"])
        wo_toks = []
        for c in range(DC):
            src = bass.AP(wo_d, c * 128 * D, [[D, 128], [1, D]])
            wo_toks.append(dma("gpsimd", C_w8k[:, c * 1024:(c + 1) * 1024], src,
                               waits=[head_T_done] + vasnap()))
        zT_ev = {}
        for t in range(NB):
            for c in range(DC):
                bk = (t * DC + c) % 4
                ttr = transpose(
                    bank(bk, 128),
                    arenaB[:, t * 1024 + c * 128: t * 1024 + c * 128 + 128],
                    waits=[zmult_toks[-1], bankc[bk]])
                dstA = arenaA[:, c * 2048 + t * 128: c * 2048 + (t + 1) * 128]
                if c % 2 == 0:
                    tev = vop("tensor_copy", dstA, bank(bk, 128), waits=[ttr],
                              inc=True)
                else:
                    tev = aop("copy", dstA, bank(bk, 128), waits=[ttr], inc=True)
                bankc[bk] = tev
                zT_ev[t] = tev
        zT_ready = vasnap()

        x1_toks = [None] * NB
        sq_toks = [None] * NB
        for t in range(NB):
            pr = t % 2
            bk0 = 4 + 2 * pr
            wts = zT_ready + wo_toks + [bankc[bk0], bankc[bk0 + 1]]
            for c in range(DC):
                for hh in range(2):
                    mm(bank(bk0 + hh),
                       arenaA[:, c * 2048 + t * 128: c * 2048 + (t + 1) * 128],
                       C_w8k[:, c * 1024 + hh * 512: c * 1024 + hh * 512 + 512],
                       start=(c == 0), stop=(c == DC - 1),
                       waits=wts if (c == 0 and hh == 0) else (),
                       inc=(c == DC - 1 and hh == 1))
            tmm = ("T", cnt["T"])
            tsum = vop("reduce_sum", sb_st[:, ST_SUM + t:ST_SUM + t + 1],
                       bank(bk0, 1024), AX.X, waits=[tmm], inc=True)
            lnt = scratch[:, pr * 1024:(pr + 1) * 1024]
            wsq = [tmm, tsum]
            if t >= 2:
                wsq.append(x1_toks[t - 2])
            elif t < 2 and s >= 0:
                wsq.append(apply_toks[NB - 2 + t])
            tsq = aop("activation", lnt, bank(bk0, 1024), AF.Square,
                      accum_out=sb_st[:, ST_SQ + t:ST_SQ + t + 1], waits=wsq,
                      inc=True)
            tfin = stat_finish(1, t, LN_EPS, 1.0 / D, [tsum, tsq])
            tln = vop("tensor_scalar", lnt, bank(bk0, 1024), rstd_ap(t),
                      nmr_ap(t), ALU.mult, ALU.add, waits=[tfin], drain=True,
                      inc=True)
            bankc[bk0] = tln
            bankc[bk0 + 1] = tln
            xr = xrebuf[:, pr * 1024:(pr + 1) * 1024]
            wxr = [head_T_done]
            if t >= 2:
                wxr.append(x1_toks[t - 2])
                wxr.append(sq_toks[t - 2])
            else:
                wxr.append(apply_toks[NB - 2 + t])
            txr = dma("gpsimd", xr,
                      bass.AP(x_in, s * S * D + t * 128 * D, [[D, 128], [1, D]]),
                      waits=wxr)
            x1_toks[t] = vop("scalar_tensor_tensor",
                             arenaB[:, t * 1024:(t + 1) * 1024], lnt, 1.0, xr,
                             ALU.mult, ALU.add,
                             accum_out=sb_st[:, XS + ST_SUM + t:XS + ST_SUM + t + 1],
                             waits=[tln, txr, zT_ev[t]], inc=True)
            sq_toks[t] = aop("activation", xr,
                             arenaB[:, t * 1024:(t + 1) * 1024], AF.Square,
                             accum_out=sb_st[:, XS + ST_SQ + t:XS + ST_SQ + t + 1],
                             waits=[x1_toks[t]], drain=True, inc=True)

        if _PHASES < 5:
            continue
        ffn_fin = stat_finish(NB, 0, LN_EPS, 1.0 / D,
                              [x1_toks[-1], sq_toks[-1]], base=XS)
        vop("tensor_copy", sb_st[:, ST_R2:ST_R2 + NB],
            sb_st[:, XS + ST_RSTD:XS + ST_RSTD + NB], waits=[ffn_fin],
            drain=True)
        r2n2_tok = vop("tensor_copy", sb_st[:, ST_N2:ST_N2 + NB],
                       sb_st[:, XS + ST_NMR:XS + ST_NMR + NB], inc=True)

        XS = 160
        ACCB = 8192
        GREP_OFF, BREP_OFF, B2R_OFF, ONES_OFF = 12288, 13312, 14336, 15360
        ones_ap = arenaA[0:1, ONES_OFF:ONES_OFF + 128]
        tms = vop("memset", ones_ap, 1.0,
                  waits=vasnap() + [("T", cnt["T"])], drain=True, inc=True)
        rep_ready = None
        b2rep_ready = None
        for roff, rd in ((GREP_OFF, gam_d), (BREP_OFF, bet_d), (B2R_OFF, b2_d)):
            trow = dma("sync", arenaA[0:1, roff:roff + 1024],
                       bass.AP(rd, 0, [[D, 1], [1, D]]),
                       waits=vasnap() + [("T", cnt["T"])])
            for hh in range(2):
                tm = mm(bank(0), ones_ap,
                        arenaA[0:1, roff + hh * 512: roff + hh * 512 + 512],
                        start=True, stop=True,
                        waits=[trow, tms, bankc[0]], inc=True)
                rep_ready = vop("tensor_copy",
                                arenaA[:, roff + hh * 512: roff + hh * 512 + 512],
                                bank(0), waits=[tm], drain=True, inc=True)
                bankc[0] = rep_ready
        b2rep_ready = rep_ready
        grep = arenaA[:, GREP_OFF:GREP_OFF + 1024]
        brep = arenaA[:, BREP_OFF:BREP_OFF + 1024]
        b2rep = arenaA[:, B2R_OFF:B2R_OFF + 1024]

        def tile_xn2T(o, tt, prev_ltr):
            slab2 = (o % 2) * 4096
            tglob = o * TPO + tt
            x1t = arenaB[:, tglob * 1024:(tglob + 1) * 1024]
            par = tt % 2
            xnt = scratch[:, par * 1024:(par + 1) * 1024]
            wap = [x1_toks[tglob], sq_toks[tglob], r2n2_tok, prev_ltr,
                   ("V", cnt["V"]), ("A", cnt["A"])]
            wap += [odma_slot[0], odma_slot[1]]
            tap = vop("tensor_scalar", xnt, x1t, rstd_ap(tglob, base=XS),
                      nmr_ap(tglob, base=XS), ALU.mult, ALU.add,
                      waits=wap, drain=True, inc=True)
            ltr = None
            for c in range(DC):
                bk = 2 + (tt * DC + c) % 2
                ltr = transpose(bank(bk, 128), xnt[:, c * 128:(c + 1) * 128],
                                waits=[tap, bankc[bk]])
                dstA = arenaA[:, slab2 + c * OUTER + tt * 128:
                              slab2 + c * OUTER + (tt + 1) * 128]
                if c % 2 == 0:
                    tev = vop("tensor_copy", dstA, bank(bk, 128), waits=[ltr],
                              inc=True)
                else:
                    tev = aop("copy", dstA, bank(bk, 128), waits=[ltr],
                              inc=True)
                bankc[bk] = tev
            return ltr

        prev_ltr = None
        for tt0 in range(TPO):
            prev_ltr = tile_xn2T(0, tt0, prev_ltr)
        next_tr, next_ready = prev_ltr, vasnap()
        w1_tok = [None, None]
        w1_free = [None, None]
        w2_free = [None] * GRP
        hslot_free = [None] * GRP
        for o in range(NOUT):
            slab = (o % 2) * 4096
            sc = (o % 2) * 4
            xn2T_ready = next_ready

            acc_last = None
            last_consume_T = None
            for g in range(NGRP):
                h_ready = [None] * GRP
                for fi in range(2):
                    fglob = g * GRP + fi
                    w1_tok[fi % 2] = dma(
                        "sync", C_w1b[fi % 2],
                        bass.AP(w1_d, fglob * 128,
                                [[F, 128], [128 * F, DC], [1, 128]]),
                        waits=[w1_free[fi % 2]])
                for fi in range(GRP):
                    fglob = g * GRP + fi
                    wp = fi % 2
                    bk0 = fi % 2
                    wts = xn2T_ready + [w1_tok[wp], bankc[bk0]]
                    tmm_ = None
                    for c in range(DC):
                        tmm_ = mm(bank(bk0),
                                  C_w1b[wp][:, c * 128:(c + 1) * 128],
                                  arenaA[:, slab + c * OUTER:
                                         slab + c * OUTER + OUTER],
                                  start=(c == 0), stop=(c == DC - 1),
                                  waits=wts if c == 0 else (),
                                  inc=(c == DC - 1))
                    w1_free[wp] = tmm_
                    if fi + 2 < GRP:
                        w1_tok[wp] = dma(
                            "sync", C_w1b[wp],
                            bass.AP(w1_d, (fglob + 2) * 128,
                                    [[F, 128], [128 * F, DC], [1, 128]]),
                            waits=[tmm_])
                    trl = aop("activation",
                              C_hg[:, fi * OUTER:(fi + 1) * OUTER],
                              bank(bk0, OUTER), AF.Relu,
                              bias=sb_b1[:, fglob:fglob + 1],
                              waits=[tmm_, hslot_free[fi]], inc=True)
                    bankc[bk0] = trl
                    h_ready[fi] = trl
                for fi in range(GRP):
                    fglob = g * GRP + fi
                    tw2 = dma("sync", C_w2g[:, fi * 1024:(fi + 1) * 1024],
                              bass.AP(w2_d, fglob * 128 * D, [[D, 128], [1, D]]),
                              waits=[w2_free[fi]])
                    w2_free[fi] = tw2
                for hh in range(2):
                    for ns in range(OUTER // 256):
                        pp = ns % 2
                        bk0 = 4 + 2 * pp
                        wts = [bankc[bk0], bankc[bk0 + 1]] + h_ready + \
                            [w2_free[x] for x in range(GRP)]
                        tmm_ = None
                        for fi in range(GRP):
                            for nb2 in range(2):
                                tmm_ = mm(
                                    bank(bk0 + nb2),
                                    C_hg[:, fi * OUTER + (ns * 2 + nb2) * 128:
                                         fi * OUTER + (ns * 2 + nb2) * 128 + 128],
                                    C_w2g[:, fi * 1024 + hh * 512:
                                          fi * 1024 + hh * 512 + 512],
                                    start=(fi == 0), stop=(fi == GRP - 1),
                                    waits=wts if (fi == 0 and nb2 == 0) else (),
                                    inc=(fi == GRP - 1 and nb2 == 1))
                        last_consume_T = tmm_
                        for nb2 in range(2):
                            base = ACCB + (ns * 2 + nb2) * 1024 + hh * 512
                            dst = arenaA[:, base: base + 512]
                            if g == 0:
                                ev = vop("tensor_tensor", dst,
                                         bank(bk0 + nb2),
                                         b2rep[:, hh * 512: hh * 512 + 512],
                                         ALU.add,
                                         waits=[last_consume_T, b2rep_ready],
                                         inc=True)
                            else:
                                ev = vop("tensor_tensor", dst, dst,
                                         bank(bk0 + nb2), ALU.add,
                                         waits=[last_consume_T], inc=True)
                            acc_last = ev
                            bankc[bk0 + nb2] = ev
                for fi in range(GRP):
                    hslot_free[fi] = last_consume_T
                    w2_free[fi] = last_consume_T
                if o + 1 < NOUT:
                    next_tr = tile_xn2T(o + 1, g, next_tr)

            if o + 1 < NOUT:
                next_ready = vasnap()
            else:
                next_tr, next_ready = None, None

            rep_last = rep_ready
            zs = [arenaA[:, ACCB + tt * 1024: ACCB + (tt + 1) * 1024]
                  for tt in range(TPO)]
            x1s = [arenaB[:, (o * TPO + tt) * 1024:(o * TPO + tt + 1) * 1024]
                   for tt in range(TPO)]
            ots = [scratch[:, (tt % 2) * 1024:(tt % 2) * 1024 + 1024]
                   for tt in range(TPO)]
            s2t = [None] * TPO
            g1t = [None] * TPO
            for tt in range(TPO):
                vop("tensor_scalar", zs[tt], zs[tt], 1.0,
                    nmr_ap(o * TPO + tt, base=XS), ALU.mult, ALU.add,
                    waits=[acc_last, rep_last], drain=(tt == 0))
            for tt in range(TPO):
                s2t[tt] = vop("scalar_tensor_tensor", zs[tt], x1s[tt],
                              rstd_ap(o * TPO + tt, base=XS), zs[tt],
                              ALU.mult, ALU.add,
                              accum_out=sb_st[:, ST_SUM + sc + tt:
                                              ST_SUM + sc + tt + 1],
                              drain=True, inc=True)
            for tt in range(TPO):
                wg1 = [s2t[tt], next_tr, odma_slot[tt % 2]]
                g1t[tt] = vop("scalar_tensor_tensor", ots[tt], zs[tt], 1.0,
                              zs[tt], ALU.mult, ALU.mult,
                              accum_out=sb_st[:, ST_SQ + sc + tt:
                                              ST_SQ + sc + tt + 1],
                              waits=wg1, drain=True, inc=True)
            finz = stat_finish(TPO, sc, LN_EPS, 1.0 / D,
                               [s2t[-1]] + g1t)
            s8t = [None] * TPO
            g10t = [None] * TPO
            s6last = None
            for tt in range(TPO):
                s6last = vop("tensor_scalar", zs[tt], zs[tt], rstd_ap(sc + tt),
                             nmr_ap(sc + tt), ALU.mult, ALU.add, waits=[finz],
                             drain=(tt == 0), inc=(tt == TPO - 1))
            gp = None
            for tt in range(TPO):
                gp = gop("tensor_tensor", zs[tt], zs[tt], grep, ALU.mult,
                         waits=[s6last], drain=(tt == 0), inc=True)
            for tt in range(TPO):
                s8t[tt] = vop("scalar_tensor_tensor", zs[tt], zs[tt], 1.0,
                              brep, ALU.mult, ALU.add,
                              accum_out=sb_st[:, ST_SUM + 8 + sc + tt:
                                              ST_SUM + 9 + sc + tt],
                              waits=[gp], drain=True, inc=True)
            for tt in range(TPO):
                g10t[tt] = vop("scalar_tensor_tensor", ots[tt], zs[tt], 1.0,
                               zs[tt], ALU.mult, ALU.mult,
                               accum_out=sb_st[:, ST_SQ + 8 + sc + tt:
                                               ST_SQ + 9 + sc + tt],
                               waits=[s8t[tt]], drain=True, inc=True)
            finw = stat_finish(TPO, 8 + sc, LN_EPS, 1.0 / D,
                               [s8t[-1]] + g10t)
            for tt in range(TPO):
                w12 = [finw, g10t[tt], odma_slot[tt % 2]]
                t12 = vop("tensor_scalar", ots[tt], zs[tt], rstd_ap(8 + sc + tt),
                          nmr_ap(8 + sc + tt), ALU.mult, ALU.add,
                          waits=w12, drain=True, inc=True)
                t13 = vop("tensor_tensor", ots[tt], ots[tt], x1s[tt], ALU.add,
                          waits=[t12], drain=True, inc=True)
                dst = bass.AP(y_out, s * S * D + (o * OUTER + tt * 128) * D,
                              [[D, 128], [1, D]])
                out_dma_toks.append(dma("gpsimd", dst, ots[tt], waits=[t13],
                                        semname="O"))
                odma_slot[tt % 2] = out_dma_toks[-1]
        prev_seq_end = snap_all()

    if _PHASES < 5:
        t_end = snap_all()
        for s2 in range(SPC):
            for t2 in range(NB):
                d2 = bass.AP(y_out, s2 * S * D + t2 * 128 * D, [[D, 128], [1, D]])
                dma("gpsimd", d2, arenaB[:, t2 * 1024:(t2 + 1) * 1024],
                    waits=t_end, semname="O")
    fin_counts = [(k, cnt[k]) for k in OL]

    def fin(eng, fc=tuple(fin_counts)):
        for k, n in fc:
            eng.wait_ge(sem[k], n)
    ops["gpsimd"].append(fin)

    import contextlib
    with contextlib.ExitStack() as stack:
        for k in list(cnt.keys()):
            sem[k] = stack.enter_context(nc.semaphore(k))
        block = stack.enter_context(nc.Block())

        @block.sync
        def _(eng):
            for f in ops["sync"]:
                f(eng)

        @block.gpsimd
        def _(eng):
            for f in ops["gpsimd"]:
                f(eng)

        @block.tensor
        def _(eng):
            for f in ops["tensor"]:
                f(eng)

        @block.vector
        def _(eng):
            for f in ops["vector"]:
                f(eng)

        @block.scalar
        def _(eng):
            for f in ops["scalar"]:
                f(eng)

    return nc


def _decay_tiles():
    import jax
    import jax.numpy as jnp
    try:
        cpu = jax.devices("cpu")[0]
        with jax.default_device(cpu):
            gammas = 1.0 - jnp.exp2(-5.0 - jnp.arange(H, dtype=jnp.float32))
            log_g = jnp.log(gammas)
            diffs = jnp.arange(S, dtype=jnp.float32)
            dv = np.asarray(jax.device_get(
                jnp.exp(log_g[:, None] * diffs[None, :])), dtype=np.float32)
    except Exception:
        g32 = (1.0 - np.exp2(-5.0 - np.arange(H, dtype=np.float32))).astype(
            np.float32)
        lg = np.log(g32).astype(np.float32)
        dfs = np.arange(S, dtype=np.float32)
        dv = np.exp(lg[:, None] * dfs[None, :]).astype(np.float32)
    tiles = np.zeros((H, 16, 128, 512), dtype=np.float32)
    ml = np.arange(128)[:, None]
    nl = np.arange(512)[None, :]
    for ridx in range(16):
        r = ridx * 128 - 384
        d = r + nl - ml
        valid = d >= 0
        dcl = np.clip(d, 0, S - 1)
        for h in range(H):
            tiles[h, ridx] = np.where(valid, dv[h][dcl], 0.0)
    return tiles


def kernel(**inputs):
    x = np.ascontiguousarray(np.asarray(inputs["L0_attn_X"], dtype=np.float32))
    wq = np.ascontiguousarray(np.asarray(inputs["L0_attn_Wq"], dtype=np.float32))
    wk = np.ascontiguousarray(np.asarray(inputs["L0_attn_Wk"], dtype=np.float32))
    wv = np.ascontiguousarray(np.asarray(inputs["L0_attn_Wv"], dtype=np.float32))
    wg = np.ascontiguousarray(np.asarray(inputs["L0_attn_W_G"], dtype=np.float32))
    wo = np.ascontiguousarray(np.asarray(inputs["L0_attn_W_O"], dtype=np.float32))
    w1 = np.ascontiguousarray(np.asarray(inputs["L0_ffn_w1"][0], dtype=np.float32))
    b1 = np.ascontiguousarray(np.asarray(inputs["L0_ffn_b1"][0], dtype=np.float32))
    w2 = np.ascontiguousarray(np.asarray(inputs["L0_ffn_w2"][0], dtype=np.float32))
    b2 = np.ascontiguousarray(np.asarray(inputs["L0_ffn_b2"][0], dtype=np.float32))
    gam = np.ascontiguousarray(np.asarray(inputs["L0_ffn_gamma"], dtype=np.float32))
    bet = np.ascontiguousarray(np.asarray(inputs["L0_ffn_beta"], dtype=np.float32))

    nc = build_program()
    dt = _decay_tiles()
    in_maps = []
    for c in range(NCORES):
        in_maps.append({
            "x": np.ascontiguousarray(x[c * SPC:(c + 1) * SPC]),
            "wq": wq, "wk": wk, "wv": wv, "wg": wg, "wo": wo,
            "w1": w1, "b1": b1, "w2": w2, "b2": b2, "gam": gam, "bet": bet,
            "dt": dt,
        })
    res = run_bass_kernel_spmd(nc, in_maps, core_ids=list(range(NCORES)))
    out = np.concatenate([r["y"] for r in res.results], axis=0)
    return out.astype(np.float32)
